# version 03
# baseline (speedup 1.0000x reference)
import math
import contextlib
import numpy as np
import concourse.bass as bass
import concourse.mybir as mybir
from concourse.bass_utils import run_bass_kernel_spmd

F32 = mybir.dt.float32
BF16 = mybir.dt.bfloat16
F16 = mybir.dt.float16
AF = mybir.ActivationFunctionType
ALU = mybir.AluOpType
AX = mybir.AxisListType

T_SEQ = 4096
D = 1024
NT = T_SEQ // 128
EPS_LN = 1e-5
LN32 = float(math.log(32.0))


class Prog:

    ENGINES = ("sync", "scalar", "vector", "tensor", "gpsimd")

    def __init__(self, nc, stack):
        self.nc = nc
        self.stack = stack
        self.steps = {e: [] for e in self.ENGINES}
        self.sems = {}
        self.counts = {}
        for name in ("t", "v", "s", "g", "dsync", "dscalar", "dgp"):
            self.sems[name] = stack.enter_context(nc.semaphore(f"sem_{name}"))
            self.counts[name] = 0
        self._waited = {e: {} for e in self.ENGINES}

    def op(self, eng, fn, inc=None):
        if inc is None:
            self.steps[eng].append(lambda e, fn=fn: fn(e))
            return None
        sem = self.sems[inc]
        self.counts[inc] += 1
        cnt = self.counts[inc]

        def run(e, fn=fn, sem=sem):
            fn(e).then_inc(sem, 1)
        self.steps[eng].append(run)
        return cnt

    def dma(self, queue, out, in_, **kw):
        sname = {"sync": "dsync", "scalar": "dscalar", "gpsimd": "dgp"}[queue]
        sem = self.sems[sname]
        self.counts[sname] += 16
        cnt = self.counts[sname]

        def run(e, out=out, in_=in_, kw=kw, sem=sem):
            e.dma_start(out=out, in_=in_, **kw).then_inc(sem, 16)
        self.steps[queue].append(run)
        return cnt

    def wait(self, eng, sem_name, count):
        if count is None:
            return
        prev = self._waited[eng].get(sem_name, -1)
        if count <= prev:
            return
        self._waited[eng][sem_name] = count
        sem = self.sems[sem_name]
        self.steps[eng].append(lambda e, sem=sem, count=count: e.wait_ge(sem, count))

    def barrier(self):
        for eng in self.ENGINES:
            for s in ("t", "v", "s", "g", "dsync", "dscalar", "dgp"):
                self.wait(eng, s, self.counts[s] if self.counts[s] > 0 else None)

    def emit(self):
        with self.nc.Block() as block:
            for name in self.ENGINES:
                steps = self.steps[name]
                if not steps:
                    continue

                def make(steps):
                    def run(eng):
                        for fn in steps:
                            fn(eng)
                    return run
                getattr(block, name)(make(steps))


_PFX = [""]


def alloc(stack, nc, name, shape, dtype):
    return stack.enter_context(nc.sbuf_tensor(_PFX[0] + name, shape, dtype))


def alloc_psum(stack, nc, name, shape, dtype=F32):
    return stack.enter_context(nc.psum_tensor(_PFX[0] + name, shape, dtype))


def stage_decl(nc, io, pfx, decls_out):
    def decl(name, shape, kind="ExternalInput", dt=F32):
        if io is not None and name in io:
            decls_out[name] = io[name]
            return io[name]
        ap = nc.dram_tensor(pfx + name, shape, dt, kind=kind).ap()
        decls_out[name] = ap
        return ap
    return decl


class Consts:

    def __init__(self, stack, nc, p):
        self.stack, self.nc, self.p = stack, nc, p
        self.cache = {}

    def get(self, val, consumer_eng):
        val = float(val)
        if val not in self.cache:
            t = alloc(self.stack, self.nc, f"const_{len(self.cache)}", [128, 1], F32)
            cnt = self.p.op("gpsimd",
                            lambda e, t=t, val=val: e.memset(t[:], val), inc="g")
            self.cache[val] = (t, cnt)
        t, cnt = self.cache[val]
        self.p.wait(consumer_eng, "g", cnt)
        return t[:]


def make_identity(p, ident_bf):
    p.op("gpsimd", lambda e: e.memset(ident_bf[:], 1.0))
    return p.op("gpsimd", lambda e: e.affine_select(
        ident_bf[:], ident_bf[:], [[-1, 128]], ALU.is_equal, 0.0,
        base=0, channel_multiplier=1), inc="g")


class LNState:
    def __init__(self, stack, nc, tag, scr_ext=None, xc_ext=None):
        self.scr = scr_ext if scr_ext is not None else alloc(
            stack, nc, f"ln_scr_{tag}", [128, D], F32)
        self.xc = xc_ext if xc_ext is not None else alloc(
            stack, nc, f"ln_xc_{tag}", [128, D], F32)
        self.s1 = alloc(stack, nc, f"ln_s1_{tag}", [128, 1], F32)
        self.s2 = alloc(stack, nc, f"ln_s2_{tag}", [128, 1], F32)
        self.mean = alloc(stack, nc, f"ln_mean_{tag}", [128, 1], F32)
        self.t1 = alloc(stack, nc, f"ln_t1_{tag}", [128, 1], F32)
        self.va = alloc(stack, nc, f"ln_va_{tag}", [128, 1], F32)
        self.lnv = alloc(stack, nc, f"ln_lnv_{tag}", [128, 1], F32)
        self.rstd = alloc(stack, nc, f"ln_rstd_{tag}", [128, 1], F32)


def emit_layernorm(p, st, xt, outs, consts):
    c_eps = consts.get(D * EPS_LN, "scalar")
    c_ln32 = consts.get(LN32, "scalar")
    xt = xt[:]
    scr = st.scr[:]
    p.op("vector", lambda e: e.reduce_sum(st.s1[:], xt, axis=AX.X))
    p.op("vector", lambda e: e.tensor_mul(scr, xt, xt))
    c = p.op("vector", lambda e: e.reduce_sum(st.s2[:], scr, axis=AX.X), inc="v")
    p.wait("vector", "v", c)
    p.op("vector", lambda e: e.tensor_scalar(st.mean[:], st.s1[:], 1.0 / D, None, ALU.mult))
    c = p.op("vector", lambda e: e.tensor_scalar(
        st.t1[:], st.s1[:], st.s1[:], 1.0 / D, ALU.mult, ALU.mult), inc="v")
    p.wait("vector", "v", c)
    xcap = st.xc[:]
    p.op("vector", lambda e: e.tensor_scalar(
        xcap, xt, st.mean[:], None, ALU.subtract))
    c = p.op("vector", lambda e: e.scalar_tensor_tensor(
        st.va[:], st.t1[:], -1.0, st.s2[:], ALU.mult, ALU.add), inc="v")
    p.wait("scalar", "v", c)
    cs = p.op("scalar", lambda e: e.activation(
        st.lnv[:], st.va[:], AF.Ln, bias=c_eps), inc="s")
    p.wait("scalar", "s", cs)
    cs = p.op("scalar", lambda e: e.activation(
        st.rstd[:], st.lnv[:], AF.Exp, scale=-0.5, bias=c_ln32), inc="s")
    p.wait("scalar", "s", cs)
    for out in outs:
        cs = p.op("scalar", lambda e, out=out, xcap=xcap: e.activation(
            out, xcap, AF.Copy, scale=st.rstd[:]), inc="s")
    return cs


class Transposer:

    def __init__(self, stack, nc, p, ident, tag="", dtype=BF16, ps_ext=None):
        self.p = p
        self.ident = ident
        if ps_ext is not None:
            self.ps = ps_ext
        else:
            self.ps = alloc_psum(stack, nc, f"ps_tr{tag}", [128, 1024], dtype)
        self.last_cv = None

    def run(self, src_ap_fn, nchunks, dst, extra_waits=()):
        p = self.p
        if callable(dst):
            dst_fn = dst
        else:
            def dst_fn(g0, gn, dst=dst):
                return dst[:, g0 * 128:(g0 + gn) * 128].rearrange(
                    "p (c n) -> p c n", n=128)
        done = None
        for g0 in range(0, nchunks, 8):
            gn = min(8, nchunks - g0)
            p.wait("tensor", "v", self.last_cv)
            for sem, cnt in extra_waits:
                p.wait("tensor", sem, cnt)
            ct = None
            for i in range(gn):
                fn = lambda e, c=g0 + i, i=i: e.transpose(
                    self.ps[:, i * 128:(i + 1) * 128], src_ap_fn(c), self.ident[:])
                ct = p.op("tensor", fn, inc="t" if i == gn - 1 else None)
            p.wait("vector", "t", ct)
            cv = p.op("vector", lambda e, g0=g0, gn=gn: e.tensor_copy(
                dst_fn(g0, gn),
                self.ps[:, 0:gn * 128].rearrange("p (c n) -> p c n", n=128)),
                inc="v")
            self.last_cv = cv
            done = cv
        return done



def build_mlp_ffn(nc, dh=4096, coef=0.5, fused=None, io=None, pfx=""):
    decls = {}
    decl = stage_decl(nc, io, pfx, decls)
    """x_in/x_out [4096,1024] f32; weights w1 [1024,dh] f32, b1 [dh],
    w2 [dh,1024], b2 [1024] (b2 already includes any extra bias)."""
    x_in = decl("x_in", [T_SEQ, D], "ExternalInput", F32)
    w1 = decl("w1", [D, dh], "ExternalInput", F32)
    b1 = decl("b1", [dh], "ExternalInput", F32)
    w2 = decl("w2", [dh, D], "ExternalInput", F32)
    b2 = decl("b2", [D], "ExternalInput", F32)
    x_out = decl("x_out", [T_SEQ, D], "ExternalOutput", F32)

    with contextlib.ExitStack() as stack:
        _PFX[0] = pfx
        if fused is None:
            p = Prog(nc, stack)
        else:
            p = fused
            p.barrier()
        w1b = alloc(stack, nc, "w1b", [128, 8, dh], BF16)
        w2b = alloc(stack, nc, "w2b", [128, dh // 128, D], BF16)
        b1r = alloc(stack, nc, "b1r", [1, dh], BF16)
        b2r = alloc(stack, nc, "b2r", [1, D], BF16)
        onesr = alloc(stack, nc, "onesr", [1, 128], BF16)
        ident = alloc(stack, nc, "ident", [128, 128], BF16)
        ln = LNState(stack, nc, "ffn")
        xt = [alloc(stack, nc, f"xt{i}", [128, D], F32) for i in range(2)]
        xn = [alloc(stack, nc, f"xn{i}", [128, D], BF16) for i in range(2)]
        xnT = [alloc(stack, nc, f"xnT{i}", [128, 8 * 128], BF16) for i in range(2)]
        hb = alloc(stack, nc, "hb", [128, 1024], BF16)
        hT = alloc(stack, nc, "hT", [128, 8 * 128], BF16)
        yt = [alloc(stack, nc, f"yt{i}", [128, D], F32) for i in range(2)]
        ps_h = [alloc_psum(stack, nc, f"ps_h{i}", [128, 1024], F32) for i in range(2)]
        ps_y = [alloc_psum(stack, nc, f"ps_y{i}", [128, 512], F32) for i in range(2)]

        p.dma("gpsimd", w1b.ap(), w1.rearrange("(c p) m -> p c m", p=128))
        p.dma("gpsimd", w2b.ap(), w2.rearrange("(c p) m -> p c m", p=128))
        p.dma("gpsimd", b1r.ap(), b1.rearrange("(o m) -> o m", o=1))
        cwb = p.dma("gpsimd", b2r.ap(), b2.rearrange("(o m) -> o m", o=1))
        p.op("gpsimd", lambda e: e.memset(onesr[:], 1.0))
        cones = make_identity(p, ident)
        consts = Consts(stack, nc, p)
        p.wait("tensor", "dgp", cwb)
        p.wait("tensor", "g", cones)

        tr = Transposer(stack, nc, p, ident)
        evac_v = [None] * NT
        out_dma = [None] * NT
        NQ = dh // 1024
        gelu_s = [None, None]

        for t in range(NT):
            b = t % 2
            if t >= 2:
                p.wait("sync", "v", evac_v[t - 2])
            cx = p.dma("sync", xt[b][:], x_in[t * 128:(t + 1) * 128, :])
            p.wait("vector", "dsync", cx)
            cs = emit_layernorm(p, ln, xt[b], [xn[b][:]], consts)
            cv_xnT = tr.run(lambda c, b=b: xn[b][:, c * 128:(c + 1) * 128],
                            8, xnT[b][:], extra_waits=[("s", cs)])
            for q in range(NQ):
                hp = ps_h[q % 2]
                p.wait("tensor", "v", cv_xnT)
                p.wait("tensor", "s", gelu_s[q % 2])
                cm = None
                for fb in range(2):
                    for c in range(8):
                        def mmfn(e, c=c, fb=fb, q=q, b=b, hp=hp):
                            return e.matmul(
                                hp[:, fb * 512:(fb + 1) * 512],
                                xnT[b][:, c * 128:(c + 1) * 128],
                                w1b[:, c, q * 1024 + fb * 512:
                                    q * 1024 + (fb + 1) * 512],
                                start=(c == 0), stop=False)
                        p.op("tensor", mmfn)
                    cm = p.op("tensor", lambda e, fb=fb, q=q, hp=hp: e.matmul(
                        hp[:, fb * 512:(fb + 1) * 512], onesr[:],
                        b1r[0:1, q * 1024 + fb * 512:q * 1024 + (fb + 1) * 512],
                        start=False, stop=True), inc="t" if fb == 1 else None)
                p.wait("scalar", "t", cm)
                gelu_s[q % 2] = p.op("scalar", lambda e, hp=hp: e.activation(
                    hb[:], hp[:], AF.Gelu), inc="s")
                cv_hT = tr.run(lambda c: hb[:, c * 128:(c + 1) * 128], 8, hT[:],
                               extra_waits=[("s", gelu_s[q % 2])])
                p.wait("tensor", "v", cv_hT)
                if q == 0 and t >= 1:
                    p.wait("tensor", "v", evac_v[t - 1])
                for fb in range(2):
                    for c in range(8):
                        def mm2fn(e, c=c, fb=fb, q=q):
                            return e.matmul(
                                ps_y[fb][:], hT[:, c * 128:(c + 1) * 128],
                                w2b[:, q * 8 + c, fb * 512:(fb + 1) * 512],
                                start=(q == 0 and c == 0), stop=False)
                        p.op("tensor", mm2fn)
            p.op("tensor", lambda e: e.matmul(
                ps_y[0][:], onesr[:], b2r[0:1, 0:512], start=False, stop=True))
            cm2 = p.op("tensor", lambda e: e.matmul(
                ps_y[1][:], onesr[:], b2r[0:1, 512:1024], start=False, stop=True),
                inc="t")
            p.wait("vector", "t", cm2)
            if t >= 2:
                p.wait("vector", "dsync", out_dma[t - 2])
            p.op("vector", lambda e, b=b: e.scalar_tensor_tensor(
                yt[b][:, 0:512], ps_y[0][:], coef, xt[b][:, 0:512],
                ALU.mult, ALU.add))
            evac_v[t] = p.op("vector", lambda e, b=b: e.scalar_tensor_tensor(
                yt[b][:, 512:1024], ps_y[1][:], coef, xt[b][:, 512:1024],
                ALU.mult, ALU.add), inc="v")
            p.wait("sync", "v", evac_v[t])
            out_dma[t] = p.dma("sync", x_out[t * 128:(t + 1) * 128, :], yt[b][:])
        p.wait("sync", "dsync", p.counts["dsync"])
        if fused is None:
            p.emit()
    return decls



_NC_CACHE = {}


def _get_nc(name, builder):
    if name not in _NC_CACHE:
        nc = bass.Bass("TRN2", target_bir_lowering=False)
        builder(nc)
        _NC_CACHE[name] = nc
    return _NC_CACHE[name]


def _run(name, builder, per_core_maps):
    nc = _get_nc(name, builder)
    res = run_bass_kernel_spmd(nc, per_core_maps, core_ids=list(range(8)))
    return res.results


def kernel_multi(**d):
    x = np.ascontiguousarray(d["L0_attn_x"], dtype=np.float32)
    xs = [x[b] for b in range(8)]

    fold_ins = {"W_exp": d["L0_ffn_W_exp"], "b_exp": d["L0_ffn_b_exp"],
                "upw": np.ascontiguousarray(d["L0_ffn_up_weights"]).reshape(12, 8192),
                "downw": np.ascontiguousarray(d["L0_ffn_down_weights"]).reshape(12, 8192),
                "W_con": d["L0_ffn_W_con"], "up_bias": d["L0_ffn_up_bias"],
                "b_con": d["L0_ffn_b_con"], "down_bias": d["L0_ffn_down_bias"]}
    fr = _run("l0fold", build_l0fold, [fold_ins] * 8)[0]
    w1p, b1p = np.asarray(fr["w1p"]), np.asarray(fr["b1p"])[0]
    w2p, b2p = np.asarray(fr["w2p"]), np.asarray(fr["b2p"])[0]

    def mlp(xs, w1, b1, w2, b2):
        maps = [{"x_in": xs[b], "w1": w1, "b1": b1, "w2": w2, "b2": b2}
                for b in range(8)]
        rs = _run("mlp", lambda nc: build_mlp_ffn(nc, dh=4096, coef=0.5), maps)
        return [np.asarray(rs[b]["x_out"]) for b in range(8)]

    xs = mlp(xs, w1p, b1p, w2p, b2p)
    maps = [{"x_in": xs[b], "Wq": d["L0_attn_Wq"], "Wk": d["L0_attn_Wk"],
             "Wv": d["L0_attn_Wv"], "Wo": d["L0_attn_Wo"]} for b in range(8)]
    rs = _run("l0attn", build_l0attn, maps)
    xs = [np.asarray(rs[b]["x_out"]) for b in range(8)]
    xs = mlp(xs, w1p, b1p, w2p, b2p)

    def moe(xs):
        maps = [{"x_in": xs[b], "rW": d["L1_ffn_router_W"],
                 "rb": d["L1_ffn_router_b"], "W1": d["L1_ffn_W1"],
                 "b1": d["L1_ffn_b1"], "W2": d["L1_ffn_W2"],
                 "b2": d["L1_ffn_b2"]} for b in range(8)]
        rs = _run("l1moe", build_l1moe, maps)
        return [np.asarray(rs[b]["x_out"]) for b in range(8)]
    xs = moe(xs)
    maps = [{"x_in": xs[b], "Wqkv": d["L1_attn_Wqkv"], "bqkv": d["L1_attn_bqkv"],
             "Wout": d["L1_attn_Wout"], "bout": d["L1_attn_bout"]}
            for b in range(8)]
    rs = _run("l1attn", build_l1attn, maps)
    xs = [np.asarray(rs[b]["x_out"]) for b in range(8)]
    xs = moe(xs)

    xs = mlp(xs, d["L2_ffn_w1"], d["L2_ffn_b1"], d["L2_ffn_w2"], d["L2_ffn_b2"])
    maps = [{"x_in": xs[b], "Wq": d["L2_attn_Wq"], "Wv": d["L2_attn_Wv"],
             "Wc": d["L2_attn_Wc"], "Wp1": d["L2_attn_Wp1"],
             "Wp2": d["L2_attn_Wp2"], "idv": d["L2_attn_identity"]}
            for b in range(8)]
    rs = _run("l2attn", build_l2attn, maps)
    xs = [np.asarray(rs[b]["x_out"]) for b in range(8)]
    xs = mlp(xs, d["L2_ffn_w1"], d["L2_ffn_b1"], d["L2_ffn_w2"], d["L2_ffn_b2"])

    def l3f(xs):
        maps = [{"x_in": xs[b], "Wqkv": d["L3_ffn_Wqkv"],
                 "q_scale": d["L3_ffn_q_scale"], "k_scale": d["L3_ffn_k_scale"],
                 "Wo": d["L3_ffn_Wo"], "ln_g": d["L3_ffn_ln_g"],
                 "ln_b": d["L3_ffn_ln_b"], "W1": d["L3_ffn_W1"],
                 "b1": d["L3_ffn_b1"], "W2": d["L3_ffn_W2"],
                 "b2": d["L3_ffn_b2"]} for b in range(8)]
        rs = _run("l3ffn", build_l3ffn, maps)
        return [np.asarray(rs[b]["x_out"]) for b in range(8)]
    xs = l3f(xs)
    maps = [{"x_in": xs[b], "Wq": d["L3_attn_Wq"], "bq": d["L3_attn_bq"],
             "Wk": d["L3_attn_Wk"], "bk": d["L3_attn_bk"],
             "Wv": d["L3_attn_Wv"], "bv": d["L3_attn_bv"],
             "Wo": d["L3_attn_Wo"], "bo": d["L3_attn_bo"]} for b in range(8)]
    rs = _run("l3attn", build_l3attn, maps)
    xs = [np.asarray(rs[b]["x_out"]) for b in range(8)]
    xs = l3f(xs)

    return np.stack(xs, axis=0)



def _emit_bfly_weights(p, wbuf, stage_row, onesc, ps, transpose):
    offs = [0, 1, 2, 3] if not transpose else [0, 2, 1, 3]
    rv = stage_row[0:1, :].rearrange("o (k f) -> o k f", f=4)
    cv = None
    for grp in range(2):
        ct = None
        for ci in range(2):
            comp = grp * 2 + ci
            src = rv[:, :, offs[comp]]
            for fb in range(4):
                ct = p.op("tensor", lambda e, ci=ci, src=src, fb=fb: e.matmul(
                    ps[:, ci * 2048 + fb * 512:ci * 2048 + (fb + 1) * 512],
                    onesc[:], src[:, fb * 512:(fb + 1) * 512],
                    start=True, stop=True),
                    inc="t" if (ci == 1 and fb == 3) else None)
        p.wait("vector", "t", ct)
        cv = p.op("vector", lambda e, grp=grp: e.tensor_copy(
            wbuf[:, grp * 4096:(grp + 1) * 4096], ps[:]), inc="v")
    return cv


def _emit_bfly_stage_tiles(p, cur, nxt, tmp, wbuf, ntiles, s, nparts=128):
    cv = None
    for ti in range(ntiles):
        cu = cur[0:nparts, ti, :].rearrange("p (n t s) -> p n t s", t=2, s=s)
        nx = nxt[0:nparts, ti, :].rearrange("p (n t s) -> p n t s", t=2, s=s)
        a, b = cu[:, :, 0, :], cu[:, :, 1, :]
        na, nb2 = nx[:, :, 0, :], nx[:, :, 1, :]
        w = lambda ci: wbuf[0:nparts, ci * 2048:(ci + 1) * 2048].rearrange(
            "p (n s) -> p n s", s=s)
        tm = tmp[0:nparts, :].rearrange("p (n s) -> p n s", s=s)
        p.op("vector", lambda e, na=na, a=a, ci=0: e.tensor_tensor(na, a, w(ci), ALU.mult))
        p.op("vector", lambda e, tm=tm, b=b, ci=1: e.tensor_tensor(tm, b, w(ci), ALU.mult))
        p.op("vector", lambda e, na=na, tm=tm: e.tensor_add(na, na, tm))
        p.op("vector", lambda e, nb2=nb2, a=a, ci=2: e.tensor_tensor(nb2, a, w(ci), ALU.mult))
        p.op("vector", lambda e, tm=tm, b=b, ci=3: e.tensor_tensor(tm, b, w(ci), ALU.mult))
        cv = p.op("vector", lambda e, nb2=nb2, tm=tm: e.tensor_add(nb2, nb2, tm),
                  inc="v")
    return cv


def build_l0fold(nc, fused=None, io=None, pfx=""):
    decls = {}
    decl = stage_decl(nc, io, pfx, decls)
    W_exp = decl("W_exp", [D, 4096], "ExternalInput", F32)
    b_exp = decl("b_exp", [4096], "ExternalInput", F32)
    upw = decl("upw", [12, 8192], "ExternalInput", F32)
    downw = decl("downw", [12, 8192], "ExternalInput", F32)
    W_con = decl("W_con", [4096, D], "ExternalInput", F32)
    up_bias = decl("up_bias", [4096], "ExternalInput", F32)
    b_con = decl("b_con", [D], "ExternalInput", F32)
    down_bias = decl("down_bias", [D], "ExternalInput", F32)
    w1p = decl("w1p", [D, 4096], "ExternalOutput", F32)
    b1p = decl("b1p", [1, 4096], "ExternalOutput", F32)
    w2p = decl("w2p", [4096, D], "ExternalOutput", F32)
    b2p = decl("b2p", [1, D], "ExternalOutput", F32)

    with contextlib.ExitStack() as stack:
        _PFX[0] = pfx
        if fused is None:
            p = Prog(nc, stack)
        else:
            p = fused
            p.barrier()
        wbuf = alloc(stack, nc, "wbuf", [128, 4 * 2048], F32)
        srow = alloc(stack, nc, "srow", [1, 8192], F32)
        onesc = alloc(stack, nc, "onesc", [1, 128], F32)
        tmp = alloc(stack, nc, "tmpb", [128, 2048], F32)
        y0 = alloc(stack, nc, "y0", [128, 3, 4096], F32)
        y1 = alloc(stack, nc, "y1", [128, 3, 4096], F32)
        stg = alloc(stack, nc, "stg", [128, 32, 128], F32)
        brow = alloc(stack, nc, "brow2", [1, 4096], F32)
        ident = alloc(stack, nc, "identf", [128, 128], F32)
        ps = alloc_psum(stack, nc, "psb", [128, 2048], F32)
        ps_tr = alloc_psum(stack, nc, "ps_trf", [128, 512], F32)

        p.op("gpsimd", lambda e: e.memset(onesc[:], 1.0))
        p.op("gpsimd", lambda e: e.memset(ident[:], 1.0))
        cg = p.op("gpsimd", lambda e: e.affine_select(
            ident[:], ident[:], [[-1, 128]], ALU.is_equal, 0.0,
            base=0, channel_multiplier=1), inc="g")
        p.wait("tensor", "g", cg)

        state = {"bc_t": None, "tr_v": None}

        def bcast_weights(bw_dram, stage, transpose):
            p.wait("gpsimd", "t", state["bc_t"])
            cd = p.dma("gpsimd", srow[:], bw_dram[stage].unsqueeze(0))
            p.wait("tensor", "dgp", cd)
            rv = srow[0:1, :].rearrange("o (k f) -> o k f", f=4)
            offs = [0, 1, 2, 3] if not transpose else [0, 2, 1, 3]
            cv = None
            for comp in range(4):
                src = rv[:, :, offs[comp]]
                p.wait("tensor", "v", cv)
                ct = None
                for fb in range(4):
                    ct = p.op("tensor", lambda e, src=src, fb=fb: e.matmul(
                        ps[:, fb * 512:(fb + 1) * 512], onesc[:],
                        src[:, fb * 512:(fb + 1) * 512],
                        start=True, stop=True), inc="t" if fb == 3 else None)
                state["bc_t"] = ct
                p.wait("vector", "t", ct)
                cv = p.op("vector", lambda e, comp=comp: e.tensor_copy(
                    wbuf[:, comp * 2048:(comp + 1) * 2048], ps[:]), inc="v")
            return cv

        def run_stages(bw_dram, transpose, ntiles, nparts_last):
            stages = range(11, -1, -1) if transpose else range(12)
            cur, nxt = y0, y1
            cv = None
            for stage in stages:
                s = 2 ** stage
                cw = bcast_weights(bw_dram, stage, transpose)
                for ti in range(ntiles):
                    np_ = nparts_last if ti == ntiles - 1 else 128
                    cu = cur[0:np_, ti, :].rearrange("p (n t s) -> p n t s", t=2, s=s)
                    nx = nxt[0:np_, ti, :].rearrange("p (n t s) -> p n t s", t=2, s=s)
                    a, b = cu[:, :, 0, :], cu[:, :, 1, :]
                    na, nb2 = nx[:, :, 0, :], nx[:, :, 1, :]
                    wv = [wbuf[0:np_, ci * 2048:(ci + 1) * 2048].rearrange(
                        "p (n s) -> p n s", s=s) for ci in range(4)]
                    tm = tmp[0:np_, :].rearrange("p (n s) -> p n s", s=s)
                    p.op("vector", lambda e, na=na, a=a, w=wv[0]: e.tensor_tensor(na, a, w, ALU.mult))
                    p.op("vector", lambda e, tm=tm, b=b, w=wv[1]: e.tensor_tensor(tm, b, w, ALU.mult))
                    p.op("vector", lambda e, na=na, tm=tm: e.tensor_add(na, na, tm))
                    p.op("vector", lambda e, nb2=nb2, a=a, w=wv[2]: e.tensor_tensor(nb2, a, w, ALU.mult))
                    p.op("vector", lambda e, tm=tm, b=b, w=wv[3]: e.tensor_tensor(tm, b, w, ALU.mult))
                    cv = p.op("vector", lambda e, nb2=nb2, tm=tm: e.tensor_add(nb2, nb2, tm),
                              inc="v")
                cur, nxt = nxt, cur
            return cur, cv

        last_out = None
        for grp, tlist in [(0, [0, 1, 2]), (1, [3, 4, 5]), (2, [6, 7, "bias"])]:
            cds = None
            p.wait("sync", "dsync", last_out)
            p.wait("sync", "v", state["tr_v"])
            for gi, ti in enumerate(tlist):
                if ti == "bias":
                    cds = p.dma("sync", y0[0:1, gi, :], b_exp.unsqueeze(0))
                else:
                    cds = p.dma("sync", y0[:, gi, :],
                                W_exp[ti * 128:(ti + 1) * 128, :])
            p.wait("vector", "dsync", cds)
            fin, cv = run_stages(upw, False, len(tlist),
                                 1 if tlist[-1] == "bias" else 128)
            p.wait("sync", "v", cv)
            outs = None
            for gi, ti in enumerate(tlist):
                if ti == "bias":
                    cu = p.dma("sync", brow[:], up_bias.unsqueeze(0))
                    p.wait("vector", "dsync", cu)
                    cb = p.op("vector", lambda e, gi=gi: e.tensor_add(
                        fin[0:1, gi, :], fin[0:1, gi, :], brow[:]), inc="v")
                    p.wait("sync", "v", cb)
                    outs = p.dma("sync", b1p[:], fin[0:1, gi, :])
                else:
                    outs = p.dma("sync", w1p[ti * 128:(ti + 1) * 128, :],
                                 fin[:, gi, :])
            last_out = outs
            state["tr_v"] = None

        stg_t = None
        for grp, tlist in [(0, [0, 1, 2]), (1, [3, 4, 5]), (2, [6, 7])]:
            p.wait("sync", "dsync", last_out)
            for gi, ti in enumerate(tlist):
                p.wait("sync", "t", stg_t)
                cds = p.dma("sync", stg.ap(),
                            W_con.rearrange("(c p) d -> p c d", p=128)
                            [:, :, ti * 128:(ti + 1) * 128])
                p.wait("tensor", "dsync", cds)
                for g0 in range(0, 32, 4):
                    p.wait("tensor", "v", state["tr_v"])
                    ct = None
                    for i in range(4):
                        ct = p.op("tensor", lambda e, c=g0 + i, i=i: e.transpose(
                            ps_tr[:, i * 128:(i + 1) * 128], stg[:, c, :],
                            ident[:]), inc="t" if i == 3 else None)
                    p.wait("vector", "t", ct)
                    stg_t = ct
                    state["tr_v"] = p.op("vector", lambda e, gi=gi, g0=g0: e.tensor_copy(
                        y0[:, gi, g0 * 128:(g0 + 4) * 128], ps_tr[:]), inc="v")
            fin, cv = run_stages(downw, True, len(tlist), 128)
            for gi, ti in enumerate(tlist):
                for g0 in range(0, 32, 4):
                    p.wait("tensor", "v", state["tr_v"])
                    p.wait("tensor", "v", cv)
                    ct = None
                    for i in range(4):
                        ct = p.op("tensor", lambda e, gi=gi, c=g0 + i, i=i: e.transpose(
                            ps_tr[:, i * 128:(i + 1) * 128],
                            fin[:, gi, (c) * 128:(c + 1) * 128],
                            ident[:]), inc="t" if i == 3 else None)
                    p.wait("vector", "t", ct)
                    state["tr_v"] = p.op("vector", lambda e: e.tensor_copy(
                        tmp[:, 0:512], ps_tr[:]), inc="v")
                    p.wait("sync", "v", state["tr_v"])
                    for i in range(4):
                        last_out = p.dma(
                            "sync",
                            w2p.rearrange("(c p) d -> p c d", p=128)
                            [:, g0 + i, ti * 128:(ti + 1) * 128],
                            tmp[:, i * 128:(i + 1) * 128])
                    p.wait("vector", "dsync", last_out)

        c1 = p.dma("sync", brow[0:1, 0:D], b_con.unsqueeze(0))
        c2 = p.dma("sync", brow[0:1, D:2 * D], down_bias.unsqueeze(0))
        p.wait("vector", "dsync", c2)
        cb = p.op("vector", lambda e: e.tensor_add(
            brow[0:1, 0:D], brow[0:1, 0:D], brow[0:1, D:2 * D]), inc="v")
        p.wait("sync", "v", cb)
        p.dma("sync", b2p[:], brow[0:1, 0:D])
        p.wait("sync", "dsync", p.counts["dsync"])
        if fused is None:
            p.emit()
    return decls



class FlashCtx:
    def __init__(self, stack, nc, p, tr, consts, max_klen=4096):
        self.p, self.tr, self.consts = p, tr, consts
        self.S = alloc(stack, nc, "fl_S", [128, max_klen], F32)
        self.attn = alloc(stack, nc, "fl_attn", [128, max_klen], BF16)
        self.attnT = alloc(stack, nc, "fl_attnT", [128, 1024], BF16)
        self.nm = alloc(stack, nc, "fl_nm", [128, 1], F32)
        self.l = alloc(stack, nc, "fl_l", [128, 1], F32)
        self.rl = alloc(stack, nc, "fl_rl", [128, 1], F32)
        self.ps_s = [alloc_psum(stack, nc, f"fl_ps{i}", [128, 512], F32)
                     for i in range(2)]
        self.ps_o = alloc_psum(stack, nc, "fl_po", [128, 512], F32)
        self.evac_s = [None, None]
        self.fin_v = None
        self.Z = alloc(stack, nc, "fl_Z", [128, 128], F32)
        self.mask_causal = alloc(stack, nc, "fl_mc", [128, 128], F32)
        self.mask_up_ex = alloc(stack, nc, "fl_mue", [128, 128], F32)
        self.mask_up_in = alloc(stack, nc, "fl_mui", [128, 128], F32)
        p.op("gpsimd", lambda e: e.memset(self.Z[:], 0.0))
        g1 = p.op("gpsimd", lambda e: e.affine_select(
            self.mask_causal[:], self.Z[:], [[-1, 128]], ALU.is_ge, -1e30,
            base=0, channel_multiplier=1), inc="g")
        g2 = p.op("gpsimd", lambda e: e.affine_select(
            self.mask_up_ex[:], self.Z[:], [[1, 128]], ALU.is_gt, -1e30,
            base=0, channel_multiplier=-1), inc="g")
        g3 = p.op("gpsimd", lambda e: e.affine_select(
            self.mask_up_in[:], self.Z[:], [[1, 128]], ALU.is_ge, -1e30,
            base=0, channel_multiplier=-1), inc="g")
        self.masks_g = g3
        p.wait("vector", "g", g3)


def emit_flash(fc, qT_slice, kT_cols_fn, v_tile_fn, y_out, hd, key_tiles,
               mask_pairs, extra_tensor_waits=()):
    p, tr = fc.p, fc.tr
    nkt = len(key_tiles)
    klen = nkt * 128
    evacs = []
    for g0 in range(0, nkt, 4):
        gn = min(4, nkt - g0)
        buf = (g0 // 4) % 2
        ps = fc.ps_s[buf]
        p.wait("tensor", "s", fc.evac_s[buf])
        for sem, cnt in extra_tensor_waits:
            p.wait("tensor", sem, cnt)
        cm = p.op("tensor", lambda e, g0=g0, gn=gn, ps=ps: e.matmul(
            ps[:, 0:gn * 128], qT_slice, kT_cols_fn(g0, gn * 128),
            start=True, stop=True), inc="t")
        p.wait("scalar", "t", cm)
        ce = p.op("scalar", lambda e, g0=g0, gn=gn, ps=ps: e.activation(
            fc.S[:, g0 * 128:(g0 + gn) * 128], ps[:, 0:gn * 128], AF.Copy),
            inc="s")
        fc.evac_s[buf] = ce
        evacs.append(ce)
    p.wait("vector", "s", evacs[-1])
    cmask = None
    for col_off, mask_ap in mask_pairs:
        cmask = p.op("vector", lambda e, col_off=col_off, mask_ap=mask_ap:
                     e.tensor_add(fc.S[:, col_off:col_off + 128],
                                  fc.S[:, col_off:col_off + 128], mask_ap),
                     inc="v")
    cmx = p.op("vector", lambda e: e.reduce_max(
        fc.nm[:], fc.S[:, 0:klen], axis=AX.X, negate=True), inc="v")
    p.wait("vector", "v", cmx)
    csub = p.op("vector", lambda e: e.tensor_scalar(
        fc.S[:, 0:klen], fc.S[:, 0:klen], fc.nm[:], None, ALU.add), inc="v")
    p.wait("scalar", "v", csub)
    cexp = p.op("scalar", lambda e: e.activation(
        fc.attn[:, 0:klen], fc.S[:, 0:klen], AF.Exp, accum_out=fc.l[:]),
        inc="s")
    first_av = True
    cav = None
    for g0 in range(0, nkt, 8):
        gn = min(8, nkt - g0)
        cv = tr.run(lambda c, g0=g0: fc.attn[:, (g0 + c) * 128:(g0 + c + 1) * 128],
                    gn, fc.attnT[:], extra_waits=[("s", cexp)])
        p.wait("tensor", "v", cv)
        if first_av:
            p.wait("tensor", "v", fc.fin_v)
        for i in range(gn):
            kt = key_tiles[g0 + i]
            cav = p.op("tensor", lambda e, i=i, kt=kt, first=(first_av and i == 0):
                       e.matmul(fc.ps_o[:, 0:hd],
                                fc.attnT[:, i * 128:(i + 1) * 128],
                                v_tile_fn(kt), start=first, stop=False),
                       inc="t" if g0 + i == nkt - 1 else None)
            first_av = False
    p.wait("vector", "s", cexp)
    crl = p.op("vector", lambda e: e.reciprocal(fc.rl[:], fc.l[:]), inc="v")
    p.wait("vector", "v", crl)
    p.wait("vector", "t", cav)
    cfin = p.op("vector", lambda e: e.tensor_scalar(
        y_out, fc.ps_o[:, 0:hd], fc.rl[:], None, ALU.mult), inc="v")
    fc.fin_v = cfin
    return cfin



def build_l1attn(nc, fused=None, io=None, pfx=""):
    decls = {}
    decl = stage_decl(nc, io, pfx, decls)
    x_in = decl("x_in", [T_SEQ, D], "ExternalInput", F32)
    Wqkv = decl("Wqkv", [16, D, 192], "ExternalInput", F32)
    bqkv = decl("bqkv", [16, 192], "ExternalInput", F32)
    Wout = decl("Wout", [D, D], "ExternalInput", F32)
    bout = decl("bout", [D], "ExternalInput", F32)
    x_out = decl("x_out", [T_SEQ, D], "ExternalOutput", F32)

    with contextlib.ExitStack() as stack:
        _PFX[0] = pfx
        if fused is None:
            p = Prog(nc, stack)
        else:
            p = fused
            p.barrier()
        ident = alloc(stack, nc, "ident", [128, 128], BF16)
        onesr = alloc(stack, nc, "onesr", [1, 512], BF16)
        cones = make_identity(p, ident)
        p.op("gpsimd", lambda e: e.memset(onesr[:], 1.0))
        consts = Consts(stack, nc, p)
        tr = Transposer(stack, nc, p, ident)
        fc = FlashCtx(stack, nc, p, tr, consts)
        ln = LNState(stack, nc, "l1a", scr_ext=fc.S[:, 0:D])
        xlnT = alloc(stack, nc, "xlnT", [128, 8, T_SEQ], BF16)
        y_sb = alloc(stack, nc, "y_sb", [128, NT, D], BF16)
        qT = alloc(stack, nc, "qT", [64, T_SEQ], BF16)
        kT = alloc(stack, nc, "kT", [64, T_SEQ], BF16)
        v_sb = alloc(stack, nc, "v_sb", [128, NT, 64], BF16)
        whb = [alloc(stack, nc, f"whb{i}", [128, 8, 192], BF16) for i in range(2)]
        bq_r = [alloc(stack, nc, f"bqr{i}", [1, 192], BF16) for i in range(2)]
        woutb = alloc(stack, nc, "woutb", [128, 8, D], BF16)
        bout_r = alloc(stack, nc, "boutr", [1, D], BF16)
        xt = [fc.S[:, 1024:2048], fc.S[:, 2048:3072]]
        xn = fc.attn[:, 0:D]
        yt = [fc.S[:, 0:1024], fc.S[:, 3072:4096]]
        ps_qk = alloc_psum(stack, nc, "ps_qk", [64, 512], F32)
        ps_v = alloc_psum(stack, nc, "ps_v", [128, 64], F32)
        ps_out = [alloc_psum(stack, nc, f"ps_ot{i}", [128, 512], F32)
                  for i in range(2)]
        p.wait("tensor", "g", cones)

        cv_prev = None
        for t in range(NT):
            b = t % 2
            if t >= 2:
                p.wait("sync", "v", cv_prev)
            cx = p.dma("sync", xt[b], x_in[t * 128:(t + 1) * 128, :])
            p.wait("vector", "dsync", cx)
            p.wait("vector", "s", None if t == 0 else ln_prev)
            p.wait("scalar", "v", cv_prev)
            cs = emit_layernorm(p, ln, xt[b], [xn], consts)
            cv_prev = tr.run(lambda c: xn[:, c * 128:(c + 1) * 128], 8,
                             lambda g0, gn, t=t: xlnT[:, g0:g0 + gn,
                                                      t * 128:(t + 1) * 128],
                             extra_waits=[("s", cs)])
            ln_prev = cs

        proj_done = [None, None]
        last_cfin = None
        p.wait("tensor", "v", cv_prev)
        for h in range(16):
            hb = h % 2
            p.wait("gpsimd", "t", proj_done[hb])
            cwh = p.dma("gpsimd", whb[hb].ap(),
                        Wqkv[h].rearrange("(c p) e -> p c e", p=128))
            cbh = p.dma("gpsimd", bq_r[hb].ap(), bqkv[h].unsqueeze(0))
            p.wait("tensor", "dgp", cbh)
            evq = None
            for sl in range(8):
                ts0 = sl * 512
                for which, hoff, dst, scale in ((0, 0, qT, 0.125), (1, 64, kT, 1.0)):
                    p.wait("tensor", "s", evq)
                    cmq = None
                    for c in range(8):
                        cmq = p.op("tensor", lambda e, c=c, hb=hb, hoff=hoff, ts0=ts0:
                                   e.matmul(ps_qk[:],
                                            whb[hb][:, c, hoff:hoff + 64],
                                            xlnT[:, c, ts0:ts0 + 512],
                                            start=(c == 0), stop=False), inc=None)
                    cmq = p.op("tensor", lambda e, hb=hb, hoff=hoff: e.matmul(
                        ps_qk[:], bq_r[hb][0:1, hoff:hoff + 64], onesr[:],
                        start=False, stop=True), inc="t")
                    p.wait("scalar", "t", cmq)
                    evq = p.op("scalar", lambda e, dst=dst, ts0=ts0, scale=scale:
                               e.activation(dst[0:64, ts0:ts0 + 512], ps_qk[0:64, :],
                                            AF.Copy, scale=scale), inc="s")
            evv = None
            for t in range(NT):
                p.wait("tensor", "v", evv)
                cmv = None
                for c in range(8):
                    cmv = p.op("tensor", lambda e, c=c, t=t, hb=hb: e.matmul(
                        ps_v[:], xlnT[:, c, t * 128:(t + 1) * 128],
                        whb[hb][:, c, 128:192], start=(c == 0), stop=False),
                        inc=None)
                cmv = p.op("tensor", lambda e, t=t, hb=hb: e.matmul(
                    ps_v[:], onesr[0:1, 0:128], bq_r[hb][0:1, 128:192],
                    start=False, stop=True), inc="t")
                p.wait("vector", "t", cmv)
                evv = p.op("vector", lambda e, t=t: e.tensor_copy(
                    v_sb[:, t, :], ps_v[:]), inc="v")
            proj_done[hb] = cmv
            for qi in range(NT):
                key_tiles = list(range(qi + 1))
                cfin = emit_flash(
                    fc,
                    qT[0:64, qi * 128:(qi + 1) * 128],
                    lambda g0, ncols, qi=qi: kT[0:64, g0 * 128:g0 * 128 + ncols],
                    lambda kt: v_sb[:, kt, :],
                    y_sb[:, qi, h * 64:(h + 1) * 64],
                    64, key_tiles,
                    [(qi * 128, fc.mask_causal[:])],
                    extra_tensor_waits=[("s", evq), ("v", evv)])
                last_cfin = cfin
        cw = p.dma("gpsimd", woutb.ap(), Wout.rearrange("(c p) m -> p c m", p=128))
        cb = p.dma("gpsimd", bout_r.ap(), bout.unsqueeze(0))
        p.wait("tensor", "dgp", cb)
        out_dma = [None] * NT
        evac_v = [None] * NT
        p.wait("sync", "v", last_cfin)
        for t in range(NT):
            b = t % 2
            if t >= 2:
                p.wait("sync", "v", evac_v[t - 2])
                p.wait("sync", "dsync", out_dma[t - 2])
            cx = p.dma("sync", xt[b][:], x_in[t * 128:(t + 1) * 128, :])
            cv = tr.run(lambda c, t=t: y_sb[:, t, c * 128:(c + 1) * 128], 8,
                        fc.attnT[:], extra_waits=[("v", last_cfin)])
            p.wait("tensor", "v", cv)
            if t >= 1:
                p.wait("tensor", "v", evac_v[t - 1])
            cm2 = None
            for fb in range(2):
                for c in range(8):
                    p.op("tensor", lambda e, c=c, fb=fb: e.matmul(
                        ps_out[fb][:], fc.attnT[:, c * 128:(c + 1) * 128],
                        woutb[:, c, fb * 512:(fb + 1) * 512],
                        start=(c == 0), stop=False))
                cm2 = p.op("tensor", lambda e, fb=fb: e.matmul(
                    ps_out[fb][:], onesr[0:1, 0:128],
                    bout_r[0:1, fb * 512:(fb + 1) * 512],
                    start=False, stop=True), inc="t")
            p.wait("vector", "t", cm2)
            p.wait("vector", "dsync", cx)
            p.op("vector", lambda e, b=b: e.tensor_add(
                yt[b][:, 0:512], ps_out[0][:], xt[b][:, 0:512]))
            evac_v[t] = p.op("vector", lambda e, b=b: e.tensor_add(
                yt[b][:, 512:1024], ps_out[1][:], xt[b][:, 512:1024]), inc="v")
            p.wait("sync", "v", evac_v[t])
            out_dma[t] = p.dma("sync", x_out[t * 128:(t + 1) * 128, :], yt[b])
        p.wait("sync", "dsync", p.counts["dsync"])
        if fused is None:
            p.emit()
    return decls



def emit_rope_tables(p, nc, stack, consts, cos_all, sin_all, ps_ang, thr, itile,
                     ftile, fint, state):
    for t in range(NT):
        trow = state["trow"]
        p.wait("gpsimd", "t", state.get("trow_used"))
        cg = p.op("gpsimd", lambda e, t=t: e.iota(
            trow[:], [[1, 128]], base=t * 128, channel_multiplier=0,
            allow_small_or_imprecise_dtypes=True), inc="g")
        p.wait("tensor", "g", cg)
        p.wait("tensor", "v", state.get("ang_used"))
        cm = p.op("tensor", lambda e: e.matmul(
            ps_ang[:], trow[:], thr[:], start=True, stop=True), inc="t")
        state["trow_used"] = cm
        p.wait("vector", "t", cm)
        for which, off, dst in (("sin", 0.0, sin_all), ("cos", 0.25, cos_all)):
            if off:
                c0 = p.op("vector", lambda e, off=off: e.tensor_scalar(
                    ftile[:, 0:64], ps_ang[:], off, None, ALU.add), inc="v")
            else:
                c0 = p.op("vector", lambda e: e.tensor_copy(
                    ftile[:, 0:64], ps_ang[:]), inc="v")
            p.wait("vector", "v", c0)
            c1 = p.op("vector", lambda e: e.tensor_copy(fint[:], ftile[:, 0:64]), inc="v")
            p.wait("vector", "v", c1)
            c2 = p.op("vector", lambda e: e.tensor_copy(
                ftile[:, 64:128], fint[:]), inc="v")
            p.wait("vector", "v", c2)
            c3 = p.op("vector", lambda e: e.tensor_sub(
                ftile[:, 0:64], ftile[:, 0:64], ftile[:, 64:128]), inc="v")
            p.wait("scalar", "v", c3)
            cs = p.op("scalar", lambda e, t=t, dst=dst: e.activation(
                dst[:, t, :], ftile[:, 0:64], AF.Sin,
                scale=float(2 * math.pi)), inc="s")
            p.wait("vector", "s", cs)
        state["ang_used"] = c0
    return cs


def emit_rope_apply(p, src_f32, dst_bf, cos_t, sin_t, tmp):
    sv = src_f32.rearrange("p (j t) -> p j t", t=2)
    dv = dst_bf.rearrange("p (j t) -> p j t", t=2)
    tv = tmp.rearrange("p (j t) -> p j t", t=2)
    E, O = sv[:, :, 0], sv[:, :, 1]
    p.op("vector", lambda e: e.tensor_tensor(tv[:, :, 0], E, cos_t, ALU.mult))
    p.op("vector", lambda e: e.tensor_tensor(tv[:, :, 1], O, sin_t, ALU.mult))
    p.op("vector", lambda e: e.tensor_sub(dv[:, :, 0], tv[:, :, 0], tv[:, :, 1]))
    p.op("vector", lambda e: e.tensor_tensor(tv[:, :, 0], O, cos_t, ALU.mult))
    p.op("vector", lambda e: e.tensor_tensor(tv[:, :, 1], E, sin_t, ALU.mult))
    return p.op("vector", lambda e: e.tensor_add(
        dv[:, :, 1], tv[:, :, 0], tv[:, :, 1]), inc="v")


def build_l0attn(nc, fused=None, io=None, pfx=""):
    decls = {}
    decl = stage_decl(nc, io, pfx, decls)
    x_in = decl("x_in", [T_SEQ, D], "ExternalInput", F32)
    Wq = decl("Wq", [2, 128, D], "ExternalInput", F32)
    Wk = decl("Wk", [128, D], "ExternalInput", F32)
    Wv = decl("Wv", [128, D], "ExternalInput", F32)
    Wo = decl("Wo", [D, 256], "ExternalInput", F32)
    x_out = decl("x_out", [T_SEQ, D], "ExternalOutput", F32)

    with contextlib.ExitStack() as stack:
        _PFX[0] = pfx
        if fused is None:
            p = Prog(nc, stack)
        else:
            p = fused
            p.barrier()
        ident = alloc(stack, nc, "ident", [128, 128], BF16)
        onesr = alloc(stack, nc, "onesr", [1, 512], BF16)
        cones = make_identity(p, ident)
        p.op("gpsimd", lambda e: e.memset(onesr[:], 1.0))
        consts = Consts(stack, nc, p)
        tr = Transposer(stack, nc, p, ident)
        fc = FlashCtx(stack, nc, p, tr, consts)
        ln = LNState(stack, nc, "l0a", scr_ext=fc.S[:, 0:D])
        xlnT = alloc(stack, nc, "xlnT", [128, 8, T_SEQ], BF16)
        y_sb = alloc(stack, nc, "y_sb", [128, NT, 256], BF16)
        qT = [alloc(stack, nc, f"qT{h}", [128, T_SEQ], BF16) for h in range(2)]
        kT = alloc(stack, nc, "kTT", [128, T_SEQ], BF16)
        v_sb = alloc(stack, nc, "v_sb", [128, NT, 128], BF16)
        cos_all = alloc(stack, nc, "cos_all", [128, NT, 64], F32)
        sin_all = alloc(stack, nc, "sin_all", [128, NT, 64], F32)
        wstage = alloc(stack, nc, "wstage", [128, 1024], BF16)
        whT = alloc(stack, nc, "whT", [128, 8, 512], BF16)
        woT = alloc(stack, nc, "woT", [128, 2, D], BF16)
        thr = alloc(stack, nc, "thr", [1, 64], F32)
        trow = alloc(stack, nc, "trow", [1, 128], F32)
        jrow = alloc(stack, nc, "jrow", [1, 64], F32)
        ftile = alloc(stack, nc, "ftile", [128, 128], F32)
        fint = alloc(stack, nc, "fint", [128, 64], mybir.dt.int32)
        qkvf = alloc(stack, nc, "qkvf", [128, 512], F32)
        rtmp = alloc(stack, nc, "rtmp", [128, 128], F32)
        qkvb = alloc(stack, nc, "qkvb", [128, 384], BF16)
        xt = [fc.S[:, 1024:2048], fc.S[:, 2048:3072]]
        xn = fc.attn[:, 0:D]
        yt = [fc.S[:, 0:1024], fc.S[:, 3072:4096]]
        ps_qkv = [alloc_psum(stack, nc, f"ps_qkv{i}", [128, 512], F32)
                  for i in range(2)]
        ps_ang = alloc_psum(stack, nc, "ps_ang", [128, 64], F32)
        p.wait("tensor", "g", cones)

        cj = p.op("gpsimd", lambda e: e.iota(
            jrow[:], [[1, 64]], base=0, channel_multiplier=0,
            allow_small_or_imprecise_dtypes=True), inc="g")
        p.wait("scalar", "g", cj)
        cln2pi = consts.get(-math.log(2 * math.pi), "scalar")
        cth = p.op("scalar", lambda e: e.activation(
            thr[:], jrow[:], AF.Exp, scale=float(-math.log(1e4) / 32.0),
            bias=cln2pi[0:1, :]), inc="s")
        p.wait("tensor", "s", cth)

        state = {"trow": trow}
        emit_rope_tables(p, nc, stack, consts, cos_all, sin_all, ps_ang, thr,
                         None, ftile, fint, state)

        wlist = [Wq[0], Wq[1], Wk, Wv]
        for wi, wsrc in enumerate(wlist):
            cd = p.dma("gpsimd", wstage.ap(), wsrc)
            cv = tr.run(lambda c: wstage[:, c * 128:(c + 1) * 128], 8,
                        lambda g0, gn, wi=wi: whT[:, g0:g0 + gn,
                                                  wi * 128:(wi + 1) * 128],
                        extra_waits=[("dgp", cd)])
            p.wait("gpsimd", "v", cv)
        for c_d in range(8):
            cd = p.dma("gpsimd", wstage[:, 0:256],
                       Wo[c_d * 128:(c_d + 1) * 128, :])
            cv = tr.run(lambda c: wstage[:, c * 128:(c + 1) * 128], 2,
                        lambda g0, gn, c_d=c_d: woT[:, g0:g0 + gn,
                                                    c_d * 128:(c_d + 1) * 128],
                        extra_waits=[("dgp", cd)])
            p.wait("gpsimd", "v", cv)

        cv_prev = None
        ln_prev = None
        ev_prev = None
        cq_prev = None
        for t in range(NT):
            b = t % 2
            if t >= 2:
                p.wait("sync", "v", cv_prev)
            cx = p.dma("sync", xt[b], x_in[t * 128:(t + 1) * 128, :])
            p.wait("vector", "dsync", cx)
            p.wait("vector", "s", ln_prev)
            p.wait("scalar", "v", cv_prev)
            cs = emit_layernorm(p, ln, xt[b], [xn], consts)
            cv_prev = tr.run(lambda c: xn[:, c * 128:(c + 1) * 128], 8,
                             lambda g0, gn, t=t: xlnT[:, g0:g0 + gn,
                                                      t * 128:(t + 1) * 128],
                             extra_waits=[("s", cs)])
            ln_prev = cs
            pq = ps_qkv[t % 2]
            p.wait("tensor", "v", cv_prev)
            p.wait("tensor", "s", ev_prev)
            cm = None
            for c in range(8):
                cm = p.op("tensor", lambda e, c=c, t=t, pq=pq: e.matmul(
                    pq[:], xlnT[:, c, t * 128:(t + 1) * 128], whT[:, c, :],
                    start=(c == 0), stop=(c == 7)), inc="t" if c == 7 else None)
            p.wait("scalar", "t", cm)
            p.wait("scalar", "v", cq_prev)
            ev0 = p.op("scalar", lambda e, pq=pq: e.activation(
                qkvf[:, 0:256], pq[:, 0:256], AF.Copy,
                scale=float(1.0 / math.sqrt(128.0))), inc="s")
            ev_prev = p.op("scalar", lambda e, pq=pq: e.activation(
                qkvf[:, 256:512], pq[:, 256:512], AF.Copy), inc="s")
            p.wait("vector", "s", ev_prev)
            for wi, dst in ((0, qkvb[:, 0:128]), (1, qkvb[:, 128:256]),
                            (2, qkvb[:, 256:384])):
                emit_rope_apply(p, qkvf[:, wi * 128:(wi + 1) * 128], dst,
                                cos_all[:, t, :], sin_all[:, t, :], rtmp[:])
            cvv = p.op("vector", lambda e, t=t: e.tensor_copy(
                v_sb[:, t, :], qkvf[:, 384:512]), inc="v")
            cq = tr.run(lambda c: qkvb[:, c * 128:(c + 1) * 128], 3,
                        lambda g0, gn, t=t: (
                            qT[0][:, t * 128:(t + 1) * 128].unsqueeze(1)
                            if (g0, gn) == (0, 3) else None),
                        extra_waits=[("v", cvv)]) if False else None
            for wi, dstb in ((0, qT[0]), (1, qT[1]), (2, kT)):
                cq = tr.run(lambda c, wi=wi: qkvb[:, wi * 128:(wi + 1) * 128], 1,
                            lambda g0, gn, dstb=dstb, t=t:
                            dstb[:, t * 128:(t + 1) * 128].unsqueeze(1),
                            extra_waits=[("v", cvv)])
            cq_prev = cq
        last_cfin = None
        for h in range(2):
            for qi in range(NT):
                key_tiles = list(range(qi + 1))
                cfin = emit_flash(
                    fc,
                    qT[h][:, qi * 128:(qi + 1) * 128],
                    lambda g0, ncols: kT[:, g0 * 128:g0 * 128 + ncols],
                    lambda kt: v_sb[:, kt, :],
                    y_sb[:, qi, h * 128:(h + 1) * 128],
                    128, key_tiles,
                    [(qi * 128, fc.mask_causal[:])],
                    extra_tensor_waits=[("v", cq), ("s", ev_prev)])
                last_cfin = cfin
        out_dma = [None] * NT
        evac_v = [None] * NT
        p.wait("sync", "v", last_cfin)
        for t in range(NT):
            b = t % 2
            if t >= 2:
                p.wait("sync", "v", evac_v[t - 2])
                p.wait("sync", "dsync", out_dma[t - 2])
            cx = p.dma("sync", xt[b], x_in[t * 128:(t + 1) * 128, :])
            cv = tr.run(lambda c, t=t: y_sb[:, t, c * 128:(c + 1) * 128], 2,
                        fc.attnT[:], extra_waits=[("v", last_cfin)])
            p.wait("tensor", "v", cv)
            if t >= 1:
                p.wait("tensor", "v", evac_v[t - 1])
            cm2 = None
            for fb in range(2):
                for c in range(2):
                    cm2 = p.op("tensor", lambda e, c=c, fb=fb: e.matmul(
                        ps_qkv[fb][:], fc.attnT[:, c * 128:(c + 1) * 128],
                        woT[:, c, fb * 512:(fb + 1) * 512],
                        start=(c == 0), stop=(c == 1)),
                        inc="t" if c == 1 else None)
            p.wait("vector", "t", cm2)
            p.wait("vector", "dsync", cx)
            p.op("vector", lambda e, b=b: e.tensor_add(
                yt[b][:, 0:512], ps_qkv[0][:], xt[b][:, 0:512]))
            evac_v[t] = p.op("vector", lambda e, b=b: e.tensor_add(
                yt[b][:, 512:1024], ps_qkv[1][:], xt[b][:, 512:1024]), inc="v")
            p.wait("sync", "v", evac_v[t])
            out_dma[t] = p.dma("sync", x_out[t * 128:(t + 1) * 128, :], yt[b])
        p.wait("sync", "dsync", p.counts["dsync"])
        if fused is None:
            p.emit()
    return decls



def build_l3attn(nc, fused=None, io=None, pfx=""):
    decls = {}
    decl = stage_decl(nc, io, pfx, decls)
    x_in = decl("x_in", [T_SEQ, D], "ExternalInput", F32)
    Wq = decl("Wq", [D, D], "ExternalInput", F32)
    bq = decl("bq", [D], "ExternalInput", F32)
    Wk = decl("Wk", [D, 256], "ExternalInput", F32)
    bk = decl("bk", [256], "ExternalInput", F32)
    Wv = decl("Wv", [D, 256], "ExternalInput", F32)
    bv = decl("bv", [256], "ExternalInput", F32)
    Wo = decl("Wo", [D, D], "ExternalInput", F32)
    bo = decl("bo", [D], "ExternalInput", F32)
    x_out = decl("x_out", [T_SEQ, D], "ExternalOutput", F32)

    with contextlib.ExitStack() as stack:
        _PFX[0] = pfx
        if fused is None:
            p = Prog(nc, stack)
        else:
            p = fused
            p.barrier()
        ident = alloc(stack, nc, "ident", [128, 128], BF16)
        onesr = alloc(stack, nc, "onesr", [1, 512], BF16)
        cones = make_identity(p, ident)
        p.op("gpsimd", lambda e: e.memset(onesr[:], 1.0))
        consts = Consts(stack, nc, p)
        tr = Transposer(stack, nc, p, ident)
        fc = FlashCtx(stack, nc, p, tr, consts)
        ln = LNState(stack, nc, "l3a", scr_ext=fc.S[:, 0:D])
        xlnT = alloc(stack, nc, "xlnT", [128, 8, T_SEQ], BF16)
        y_sb = alloc(stack, nc, "y_sb", [128, 4, D], BF16)
        qT = [alloc(stack, nc, f"qTh{h}", [64, 512], BF16) for h in range(4)]
        kT = alloc(stack, nc, "kTg", [64, 8 * 128], BF16)
        v_sb = alloc(stack, nc, "v_sbg", [128, 8, 64], BF16)
        wqb = alloc(stack, nc, "wqb", [128, 8, D], BF16)
        wkb = alloc(stack, nc, "wkb", [128, 8, 256], BF16)
        wvb = alloc(stack, nc, "wvb", [128, 8, 256], BF16)
        wob = alloc(stack, nc, "wob", [128, 8, D], BF16)
        br = alloc(stack, nc, "br", [1, D + 512 + D], BF16)
        xt = [fc.S[:, 1024:2048], fc.S[:, 2048:3072]]
        xn = fc.attn[:, 1024:2048]
        yt = [fc.S[:, 3072:4096], fc.attn[:, 2048:4096].bitcast(F32)]
        ps_p = alloc_psum(stack, nc, "ps_p", [64, 512], F32)
        ps_v = alloc_psum(stack, nc, "ps_vp", [128, 64], F32)
        ps_out = [alloc_psum(stack, nc, f"ps_o3{i}", [128, 512], F32)
                  for i in range(2)]
        p.wait("tensor", "g", cones)

        p.dma("gpsimd", wqb.ap(), Wq.rearrange("(c p) e -> p c e", p=128))
        p.dma("gpsimd", wkb.ap(), Wk.rearrange("(c p) e -> p c e", p=128))
        p.dma("gpsimd", wvb.ap(), Wv.rearrange("(c p) e -> p c e", p=128))
        p.dma("gpsimd", wob.ap(), Wo.rearrange("(c p) e -> p c e", p=128))
        p.dma("gpsimd", br[0:1, 0:D], bq.unsqueeze(0))
        p.dma("gpsimd", br[0:1, D:D + 256], bk.unsqueeze(0))
        p.dma("gpsimd", br[0:1, D + 256:D + 512], bv.unsqueeze(0))
        cwd = p.dma("gpsimd", br[0:1, D + 512:], bo.unsqueeze(0))
        p.wait("tensor", "dgp", cwd)

        cv_prev = None
        ln_prev = None
        for t in range(NT):
            b = t % 2
            if t >= 2:
                p.wait("sync", "v", cv_prev)
            cx = p.dma("sync", xt[b], x_in[t * 128:(t + 1) * 128, :])
            p.wait("vector", "dsync", cx)
            p.wait("vector", "s", ln_prev)
            p.wait("scalar", "v", cv_prev)
            cs = emit_layernorm(p, ln, xt[b], [xn], consts)
            cv_prev = tr.run(lambda c: xn[:, c * 128:(c + 1) * 128], 8,
                             lambda g0, gn, t=t: xlnT[:, g0:g0 + gn,
                                                      t * 128:(t + 1) * 128],
                             extra_waits=[("s", cs)])
            ln_prev = cs
        p.wait("tensor", "v", cv_prev)

        out_dma = []
        ev_prev = None
        evv_prev = None
        last_out_v = None
        for qb in range(8):
            qt0 = qb * 4
            kt0 = max(0, qt0 - 4)
            ktiles = list(range(kt0, qt0 + 4))
            for g in range(4):
                for si, kt in enumerate(ktiles):
                    loc = kt - kt0
                    p.wait("tensor", "s", ev_prev)
                    cm = None
                    for c in range(8):
                        cm = p.op("tensor", lambda e, c=c, g=g, kt=kt: e.matmul(
                            ps_p[:, 0:128], wkb[:, c, g * 64:(g + 1) * 64],
                            xlnT[:, c, kt * 128:(kt + 1) * 128],
                            start=(c == 0), stop=False), inc=None)
                    cm = p.op("tensor", lambda e, g=g: e.matmul(
                        ps_p[:, 0:128], br[0:1, D + g * 64:D + (g + 1) * 64],
                        onesr[0:1, 0:128], start=False, stop=True), inc="t")
                    p.wait("scalar", "t", cm)
                    ev_prev = p.op("scalar", lambda e, loc=loc: e.activation(
                        kT[0:64, loc * 128:(loc + 1) * 128], ps_p[0:64, 0:128],
                        AF.Copy), inc="s")
                    p.wait("tensor", "v", evv_prev)
                    cmv = None
                    for c in range(8):
                        cmv = p.op("tensor", lambda e, c=c, g=g, kt=kt: e.matmul(
                            ps_v[:], xlnT[:, c, kt * 128:(kt + 1) * 128],
                            wvb[:, c, g * 64:(g + 1) * 64],
                            start=(c == 0), stop=False), inc=None)
                    cmv = p.op("tensor", lambda e, g=g: e.matmul(
                        ps_v[:], onesr[0:1, 0:128],
                        br[0:1, D + 256 + g * 64:D + 256 + (g + 1) * 64],
                        start=False, stop=True), inc="t")
                    p.wait("vector", "t", cmv)
                    evv_prev = p.op("vector", lambda e, loc=loc: e.tensor_copy(
                        v_sb[:, loc, :], ps_v[:]), inc="v")
                for hh in range(4):
                    h = g * 4 + hh
                    for sl in range(1):
                        p.wait("tensor", "s", ev_prev)
                        cmq = None
                        for c in range(8):
                            cmq = p.op("tensor", lambda e, c=c, h=h, qt0=qt0:
                                       e.matmul(ps_p[:],
                                                wqb[:, c, h * 64:(h + 1) * 64],
                                                xlnT[:, c, qt0 * 128:(qt0 + 4) * 128],
                                                start=(c == 0), stop=False),
                                       inc=None)
                        cmq = p.op("tensor", lambda e, h=h: e.matmul(
                            ps_p[:], br[0:1, h * 64:(h + 1) * 64], onesr[:],
                            start=False, stop=True), inc="t")
                        p.wait("scalar", "t", cmq)
                        ev_prev = p.op("scalar", lambda e, hh=hh: e.activation(
                            qT[hh][0:64, :], ps_p[0:64, :], AF.Copy,
                            scale=0.125), inc="s")
                for hh in range(4):
                    h = g * 4 + hh
                    for ql in range(4):
                        qi = qt0 + ql
                        klo = max(0, qi - 4)
                        key_tiles = list(range(klo, qi + 1))
                        masks = [((qi - klo) * 128, fc.mask_causal[:])]
                        if qi - 4 >= 0:
                            masks.append((0, fc.mask_up_ex[:]))
                        cfin = emit_flash(
                            fc,
                            qT[hh][0:64, ql * 128:(ql + 1) * 128],
                            lambda g0, ncols, klo=klo, kt0=kt0: kT[0:64,
                                (klo - kt0 + g0) * 128:
                                (klo - kt0 + g0) * 128 + ncols],
                            lambda kt, kt0=kt0: v_sb[:, kt - kt0, :],
                            y_sb[:, ql, h * 64:(h + 1) * 64],
                            64, key_tiles, masks,
                            extra_tensor_waits=[("s", ev_prev),
                                                ("v", evv_prev)])
                        last_out_v = cfin
            for ql in range(4):
                t = qt0 + ql
                b = t % 2
                if len(out_dma) >= 2:
                    p.wait("sync", "v", last_out_v)
                    p.wait("sync", "dsync", out_dma[-2])
                cx = p.dma("sync", xt[b], x_in[t * 128:(t + 1) * 128, :])
                cv = tr.run(lambda c, ql=ql: y_sb[:, ql, c * 128:(c + 1) * 128],
                            8, fc.attnT[:], extra_waits=[("v", last_out_v)])
                p.wait("tensor", "v", cv)
                p.wait("tensor", "v", last_out_v)
                cm2 = None
                for fb in range(2):
                    for c in range(8):
                        p.op("tensor", lambda e, c=c, fb=fb: e.matmul(
                            ps_out[fb][:], fc.attnT[:, c * 128:(c + 1) * 128],
                            wob[:, c, fb * 512:(fb + 1) * 512],
                            start=(c == 0), stop=False))
                    cm2 = p.op("tensor", lambda e, fb=fb: e.matmul(
                        ps_out[fb][:], onesr[0:1, 0:128],
                        br[0:1, D + 512 + fb * 512:D + 512 + (fb + 1) * 512],
                        start=False, stop=True), inc="t")
                p.wait("vector", "t", cm2)
                p.wait("vector", "dsync", cx)
                p.op("vector", lambda e, b=b: e.tensor_add(
                    yt[b][:, 0:512], ps_out[0][:], xt[b][:, 0:512]))
                last_out_v = p.op("vector", lambda e, b=b: e.tensor_add(
                    yt[b][:, 512:1024], ps_out[1][:], xt[b][:, 512:1024]),
                    inc="v")
                p.wait("sync", "v", last_out_v)
                out_dma.append(p.dma("sync", x_out[t * 128:(t + 1) * 128, :],
                                     yt[b]))
        p.wait("sync", "dsync", p.counts["dsync"])
        if fused is None:
            p.emit()
    return decls



def build_l1moe(nc, fused=None, io=None, pfx=""):
    decls = {}
    decl = stage_decl(nc, io, pfx, decls)
    x_in = decl("x_in", [T_SEQ, D], "ExternalInput", F32)
    rW = decl("rW", [D, 8], "ExternalInput", F32)
    rb = decl("rb", [8], "ExternalInput", F32)
    W1 = decl("W1", [8, D, 4096], "ExternalInput", F32)
    b1 = decl("b1", [8, 4096], "ExternalInput", F32)
    W2 = decl("W2", [8, 4096, D], "ExternalInput", F32)
    b2 = decl("b2", [8, D], "ExternalInput", F32)
    x_out = decl("x_out", [T_SEQ, D], "ExternalOutput", F32)

    with contextlib.ExitStack() as stack:
        _PFX[0] = pfx
        if fused is None:
            p = Prog(nc, stack)
        else:
            p = fused
            p.barrier()
        ident = alloc(stack, nc, "ident", [128, 128], BF16)
        onesr = alloc(stack, nc, "onesr", [1, 512], BF16)
        cones = make_identity(p, ident)
        p.op("gpsimd", lambda e: e.memset(onesr[:], 1.0))
        consts = Consts(stack, nc, p)
        tr = Transposer(stack, nc, p, ident)
        ln = LNState(stack, nc, "moe")
        xlnT = alloc(stack, nc, "xlnT", [128, 8, T_SEQ], BF16)
        acc = alloc(stack, nc, "acc", [128, NT, D], BF16)
        w1q = alloc(stack, nc, "w1q", [128, 8, 1024], BF16)
        w2q = alloc(stack, nc, "w2q", [128, 8, 1024], BF16)
        rwb = alloc(stack, nc, "rwb", [128, 8, 8], BF16)
        rbr = alloc(stack, nc, "rbr", [1, 8], BF16)
        b1r = alloc(stack, nc, "b1r", [1, 4096], BF16)
        b2r = alloc(stack, nc, "b2r", [1, D], BF16)
        glog = alloc(stack, nc, "glog", [128, NT, 8], F32)
        gates = alloc(stack, nc, "gates", [128, NT, 8], F32)
        gtmp = alloc(stack, nc, "gtmp", [128, NT], F32)
        gtm2 = alloc(stack, nc, "gtm2", [128, NT], F32)
        found = alloc(stack, nc, "found", [128, NT], F32)
        eqb = alloc(stack, nc, "eqb", [128, NT], F32)
        selb = alloc(stack, nc, "selb", [128, NT], F32)
        xt = [alloc(stack, nc, f"xt{i}", [128, D], F32) for i in range(2)]
        xn = alloc(stack, nc, "xn", [128, D], BF16)
        hb = alloc(stack, nc, "hb", [128, 1024], BF16)
        hT = alloc(stack, nc, "hT", [128, 8 * 128], BF16)
        yt = [alloc(stack, nc, f"yt{i}", [128, D], F32) for i in range(2)]
        ps_h = [alloc_psum(stack, nc, f"ps_h{i}", [128, 1024], F32) for i in range(2)]
        ps_y = [alloc_psum(stack, nc, f"ps_y{i}", [128, 512], F32) for i in range(2)]
        ps_r = alloc_psum(stack, nc, "ps_r", [128, 8], F32)
        p.wait("tensor", "g", cones)
        cz = p.op("gpsimd", lambda e: e.memset(
            acc.ap().rearrange("p n d -> p (n d)"), 0.0), inc="g")
        p.wait("vector", "g", cz)
        cd = p.dma("gpsimd", rwb.ap(), rW.rearrange("(c p) e -> p c e", p=128))
        cd = p.dma("gpsimd", rbr.ap(), rb.unsqueeze(0))
        p.wait("tensor", "dgp", cd)

        cv_prev = None
        ln_prev = None
        ev_r = None
        for t in range(NT):
            b = t % 2
            if t >= 2:
                p.wait("sync", "v", cv_prev)
            cx = p.dma("sync", xt[b][:], x_in[t * 128:(t + 1) * 128, :])
            p.wait("vector", "dsync", cx)
            p.wait("vector", "s", ln_prev)
            p.wait("scalar", "v", cv_prev)
            cs = emit_layernorm(p, ln, xt[b][:], [xn[:]], consts)
            cv_prev = tr.run(lambda c: xn[:, c * 128:(c + 1) * 128], 8,
                             lambda g0, gn, t=t: xlnT[:, g0:g0 + gn,
                                                      t * 128:(t + 1) * 128],
                             extra_waits=[("s", cs)])
            ln_prev = cs
            p.wait("tensor", "v", cv_prev)
            p.wait("tensor", "v", ev_r)
            cm = None
            for c in range(8):
                cm = p.op("tensor", lambda e, c=c, t=t: e.matmul(
                    ps_r[:], xlnT[:, c, t * 128:(t + 1) * 128], rwb[:, c, :],
                    start=(c == 0), stop=False), inc=None)
            cm = p.op("tensor", lambda e: e.matmul(
                ps_r[:], onesr[0:1, 0:128], rbr[:],
                start=False, stop=True), inc="t")
            p.wait("vector", "t", cm)
            ev_r = p.op("vector", lambda e, t=t: e.tensor_copy(
                glog[:, t, :], ps_r[:]), inc="v")

        def vsw(fn):
            c = p.op("vector", fn, inc="v")
            p.wait("vector", "v", c)
            return c
        vsw(lambda e: e.tensor_reduce(
            gtmp[:].unsqueeze(2) if False else gtmp[:].rearrange("p (n o) -> p n o", o=1),
            glog.ap(), AX.X, ALU.max, negate=True))
        vsw(lambda e: e.tensor_tensor(
            glog.ap(), glog.ap(),
            gtmp[:].rearrange("p (n o) -> p n o", o=1).broadcast_to([128, NT, 8]),
            ALU.add))
        ce = p.op("scalar", lambda e: e.activation(
            glog.ap().rearrange("p n e -> p (n e)"),
            glog.ap().rearrange("p n e -> p (n e)"), AF.Exp), inc="s")
        p.wait("vector", "s", ce)
        vsw(lambda e: e.tensor_reduce(
            gtmp[:].rearrange("p (n o) -> p n o", o=1), glog.ap(), AX.X, ALU.add))
        vsw(lambda e: e.reciprocal(gtm2[:], gtmp[:]))
        vsw(lambda e: e.tensor_tensor(
            glog.ap(), glog.ap(),
            gtm2[:].rearrange("p (n o) -> p n o", o=1).broadcast_to([128, NT, 8]),
            ALU.mult))
        vsw(lambda e: e.tensor_reduce(
            gtmp[:].rearrange("p (n o) -> p n o", o=1), glog.ap(), AX.X, ALU.max))
        vsw(lambda e: e.memset(found[:], 0.0))
        for ex in range(8):
            vsw(lambda e, ex=ex: e.tensor_tensor(
                eqb[:], glog[:, :, ex], gtmp[:], ALU.is_ge))
            vsw(lambda e: e.scalar_tensor_tensor(
                selb[:], found[:], -1.0, eqb[:], ALU.mult, ALU.add))
            vsw(lambda e: e.tensor_relu(selb[:], selb[:]))
            vsw(lambda e, ex=ex: e.tensor_tensor(
                gates[:, :, ex], glog[:, :, ex], selb[:], ALU.mult))
            vsw(lambda e: e.tensor_add(found[:], found[:], selb[:]))

        wq_used = None
        gelu_s = [None, None]
        stt_v = None
        for ex in range(8):
            for q in range(4):
                p.wait("gpsimd", "t", wq_used)
                cd1 = p.dma("gpsimd", w1q.ap(),
                            W1[ex].rearrange("(c p) m -> p c m", p=128)
                            [:, :, q * 1024:(q + 1) * 1024])
                cd2 = p.dma("gpsimd", w2q.ap(),
                            W2[ex].rearrange("(c p) m -> p c m", p=128)
                            [:, 8 * q:8 * (q + 1), :])
                cb1 = p.dma("gpsimd", b1r[0:1, q * 1024:(q + 1) * 1024],
                            b1[ex, q * 1024:(q + 1) * 1024].unsqueeze(0))
                cb2 = p.dma("gpsimd", b2r.ap(), b2[ex].unsqueeze(0)) \
                    if q == 0 else cb1
                p.wait("tensor", "dgp", max(cd1, cd2, cb1, cb2))
                for t in range(NT):
                    hp = ps_h[t % 2]
                    p.wait("tensor", "s", gelu_s[t % 2])
                    cm = None
                    for fb in range(2):
                        for c in range(8):
                            p.op("tensor", lambda e, c=c, fb=fb, t=t, hp=hp:
                                 e.matmul(hp[:, fb * 512:(fb + 1) * 512],
                                          xlnT[:, c, t * 128:(t + 1) * 128],
                                          w1q[:, c, fb * 512:(fb + 1) * 512],
                                          start=(c == 0), stop=False))
                        cm = p.op("tensor", lambda e, fb=fb, q=q, hp=hp: e.matmul(
                            hp[:, fb * 512:(fb + 1) * 512], onesr[0:1, 0:128],
                            b1r[0:1, q * 1024 + fb * 512:q * 1024 + (fb + 1) * 512],
                            start=False, stop=True),
                            inc="t" if fb == 1 else None)
                    p.wait("scalar", "t", cm)
                    gelu_s[t % 2] = p.op("scalar", lambda e, hp=hp: e.activation(
                        hb[:], hp[:], AF.Relu), inc="s")
                    cv_hT = tr.run(lambda c: hb[:, c * 128:(c + 1) * 128], 8,
                                   hT[:], extra_waits=[("s", gelu_s[t % 2])])
                    p.wait("tensor", "v", cv_hT)
                    p.wait("tensor", "v", stt_v)
                    cm2 = None
                    for fb in range(2):
                        for c in range(8):
                            cm2 = p.op("tensor", lambda e, c=c, fb=fb, q=q:
                                       e.matmul(
                                           ps_y[fb][:],
                                           hT[:, c * 128:(c + 1) * 128],
                                           w2q[:, c, fb * 512:(fb + 1) * 512],
                                           start=(c == 0),
                                           stop=(c == 7 and q != 0)),
                                       inc="t" if (c == 7 and q != 0 and fb == 1)
                                       else None)
                        if q == 0:
                            cm2 = p.op("tensor", lambda e, fb=fb: e.matmul(
                                ps_y[fb][:], onesr[0:1, 0:128],
                                b2r[0:1, fb * 512:(fb + 1) * 512],
                                start=False, stop=True),
                                inc="t" if fb == 1 else None)
                    p.wait("vector", "t", cm2)
                    for fb in range(2):
                        stt_v = p.op("vector", lambda e, t=t, ex=ex, fb=fb:
                                     e.scalar_tensor_tensor(
                                         acc[:, t, fb * 512:(fb + 1) * 512],
                                         ps_y[fb][:], gates[:, t, ex:ex + 1],
                                         acc[:, t, fb * 512:(fb + 1) * 512],
                                         ALU.mult, ALU.add),
                                     inc="v" if fb == 1 else None)
                wq_used = cm2

        out_dma = [None] * NT
        ev_f = None
        for t in range(NT):
            b = t % 2
            if t >= 2:
                p.wait("sync", "v", ev_f)
                p.wait("sync", "dsync", out_dma[t - 2])
            cx = p.dma("sync", xt[b][:], x_in[t * 128:(t + 1) * 128, :])
            p.wait("vector", "dsync", cx)
            p.wait("vector", "v", stt_v)
            ev_f = p.op("vector", lambda e, b=b, t=t: e.scalar_tensor_tensor(
                yt[b][:], acc[:, t, :], 0.5, xt[b][:], ALU.mult, ALU.add),
                inc="v")
            p.wait("sync", "v", ev_f)
            out_dma[t] = p.dma("sync", x_out[t * 128:(t + 1) * 128, :], yt[b][:])
        p.wait("sync", "dsync", p.counts["dsync"])
        if fused is None:
            p.emit()
    return decls





NS_SCHED = ([(21.0, -1094.0, 0.0)] + [(3.4445, -4.7750, 2.0315)] * 12 +
            [(15.0 / 8, -10.0 / 8, 3.0 / 8)] * 4)
NS_TRUNC = [None]
SCAN_DEPTH = [None]
L2_NO_V = [False]
L2_NO_ID = [False]


def build_l2attn(nc, debug=False, fused=None, io=None, pfx=""):
    decls = {}
    decl = stage_decl(nc, io, pfx, decls)
    x_in = decl("x_in", [T_SEQ, D], "ExternalInput", F32)
    Wq = decl("Wq", [D, D], "ExternalInput", F32)
    Wv = decl("Wv", [D, D], "ExternalInput", F32)
    Wc = decl("Wc", [D, D], "ExternalInput", F32)
    Wp1 = decl("Wp1", [D, D], "ExternalInput", F32)
    Wp2 = decl("Wp2", [D, D], "ExternalInput", F32)
    idv = decl("idv", [D], "ExternalInput", F32)
    x_out = decl("x_out", [T_SEQ, D], "ExternalOutput", F32)
    v_scr = nc.dram_tensor(pfx + "v_scr", [128, 8, T_SEQ], BF16, kind="Internal").ap()
    if debug:
        dbg_p1 = decl("dbg_p1", [128, 8, D], "ExternalOutput", BF16)
        dbg_p2 = decl("dbg_p2", [128, 8, D], "ExternalOutput", BF16)
        dbg_q = decl("dbg_q", [128, 8, 4104], "ExternalOutput", BF16)
        dbg_l = decl("dbg_l", [128, 8, 4096], "ExternalOutput", BF16)
        dbg_w = decl("dbg_w", [128, 8, D], "ExternalOutput", BF16)
        dbg_t0 = decl("dbg_t0", [128, 8, 128], "ExternalOutput", BF16)

    NQ_COLS = 4104
    with contextlib.ExitStack() as stack:
        _PFX[0] = pfx
        if fused is None:
            p = Prog(nc, stack)
        else:
            p = fused
            p.barrier()
        ident = alloc(stack, nc, "ident", [128, 128], BF16)
        identf32 = alloc(stack, nc, "identf32", [128, 128], F32)
        cones = make_identity(p, ident)
        p.op("gpsimd", lambda e: e.memset(identf32[:], 1.0))
        ch2 = p.op("gpsimd", lambda e: e.affine_select(
            identf32[:], identf32[:], [[-1, 128]], ALU.is_equal, 0.0,
            base=0, channel_multiplier=1), inc="g")
        consts = Consts(stack, nc, p)
        tr = Transposer(stack, nc, p, ident)
        qbuf = alloc(stack, nc, "qbuf", [128, 8, NQ_COLS], BF16)
        lbuf = alloc(stack, nc, "lbuf", [128, 8, 4096], BF16)
        p1T = alloc(stack, nc, "p1T", [128, 8, D], BF16)
        p2T = alloc(stack, nc, "p2T", [128, 8, D], BF16)
        wqT = alloc(stack, nc, "wqT", [128, 8, D], BF16)
        wvT = alloc(stack, nc, "wvT", [128, 8, D], BF16)
        wstage = alloc(stack, nc, "wstage", [128, 8, 128], BF16)
        sq_bf = alloc(stack, nc, "sq_bf", [128, 2048], BF16)
        rstd_row = alloc(stack, nc, "rstd_row", [1, 256], F32)
        lnrow = alloc(stack, nc, "lnrow", [1, 256], F32)
        rstdb = alloc(stack, nc, "rstdb", [128, 256], F32)
        onescol = alloc(stack, nc, "onescol", [128, 1], BF16)
        ones1 = alloc(stack, nc, "ones1", [1, 128], F32)
        onescol_f = alloc(stack, nc, "onescol_f", [128, 1], F32)
        srow = alloc(stack, nc, "srow", [1, 8], F32)
        sval = alloc(stack, nc, "sval", [128, 1], F32)
        rs = alloc(stack, nc, "rs", [128, 1], F32)
        xnT_t = alloc(stack, nc, "xnT_t", [128, 8, 128], BF16)
        vt_t = alloc(stack, nc, "vt_t", [128, 8, 128], BF16)
        qsv_t = xnT_t
        ps = [alloc_psum(stack, nc, f"psA{i}", [128, 512], F32) for i in range(2)]
        zps = alloc_psum(stack, nc, "zps", [128, 8, 256], F32)
        pr1 = alloc_psum(stack, nc, "pr1", [1, 512], F32)

        lflat = lbuf.ap().rearrange("p c m -> p (c m)")
        qflat = qbuf.ap().rearrange("p c m -> p (c m)")
        xh = lflat[:, 0:8192].bitcast(F16)
        xTh = lflat[:, 8192:16384].bitcast(F16)
        abuf = lflat[:, 16384:24576].bitcast(F16)
        bbuf = lflat[:, 24576:32768].bitcast(F16)
        wf = qflat[:, 0:16384].bitcast(F32)
        xT2 = qflat[:, 16384:24576].bitcast(F16)

        def chs(flat, c, lo, hi):
            return flat[:, c * 1024 + lo:c * 1024 + hi]

        p.op("gpsimd", lambda e: e.memset(onescol[:], 1.0))
        p.op("gpsimd", lambda e: e.memset(ones1[:], 1.0))
        cg = p.op("gpsimd", lambda e: e.memset(onescol_f[:], 1.0), inc="g")
        p.wait("tensor", "g", cg)
        p.wait("tensor", "g", ch2)
        p.wait("vector", "g", cg)

        state = {"v": None, "s": None}

        def ns_polar(wp_dram, pT_out):
            p.wait("sync", "v", state.get("cx0"))
            cw = p.dma("sync", wf.rearrange("p (c m) -> p c m", m=1024),
                       wp_dram.rearrange("(c p) m -> p c m", p=128))
            p.wait("vector", "dsync", cw)
            p.wait("vector", "v", state["v"])
            c1 = p.op("vector", lambda e: e.tensor_mul(
                sq_bf[:], wf[:, 0:2048], wf[:, 0:2048]), inc="v")
            p.wait("vector", "v", c1)
            c2 = p.op("vector", lambda e: e.reduce_sum(
                rs[:], sq_bf[:], axis=AX.X), inc="v")
            accs = [c2]
            for sl in range(1, 4):
                p.wait("vector", "v", accs[-1])
                c1 = p.op("vector", lambda e, sl=sl: e.tensor_mul(
                    sq_bf[:], wf[:, sl * 2048:(sl + 1) * 2048],
                    wf[:, sl * 2048:(sl + 1) * 2048]), inc="v")
                p.wait("vector", "v", c1)
                c2 = p.op("vector", lambda e: e.reduce_sum(
                    sval[:], sq_bf[:], axis=AX.X), inc="v")
                p.wait("vector", "v", c2)
                c2 = p.op("vector", lambda e: e.tensor_add(
                    rs[:], rs[:], sval[:]), inc="v")
                accs.append(c2)
            p.wait("tensor", "v", accs[-1])
            cm = p.op("tensor", lambda e: e.matmul(
                pr1[0:1, 0:1], onescol_f[:], rs[:], start=True, stop=True),
                inc="t")
            p.wait("scalar", "t", cm)
            cs = p.op("scalar", lambda e: e.activation(
                srow[0:1, 0:1], pr1[0:1, 0:1], AF.Ln), inc="s")
            p.wait("scalar", "s", cs)
            cs = p.op("scalar", lambda e: e.activation(
                srow[0:1, 1:2], srow[0:1, 0:1], AF.Exp, scale=-0.5), inc="s")
            p.wait("tensor", "s", cs)
            cm = p.op("tensor", lambda e: e.matmul(
                pr1[0:1, 2:3], ones1[0:1, 0:1], srow[0:1, 1:2],
                start=True, stop=True), inc="t")
            cm = p.op("tensor", lambda e: e.matmul(
                ps[0][:, 0:1], ones1[:], srow[0:1, 1:2], start=True, stop=True),
                inc="t")
            p.wait("vector", "t", cm)
            c3 = p.op("vector", lambda e: e.tensor_copy(sval[:], ps[0][:, 0:1]),
                      inc="v")
            p.wait("vector", "v", c3)
            cx0 = p.op("vector", lambda e: e.tensor_scalar(
                xh[:], wf[:], sval[:], None, ALU.mult), inc="v")
            state["v"] = cx0
            state["cx0"] = cx0
            cvt = None
            for jc in range(8):
                for g0 in range(0, 8, 4):
                    p.wait("tensor", "v", state["v"])
                    ct = None
                    for i in range(4):
                        kc = g0 + i
                        ct = p.op("tensor", lambda e, kc=kc, jc=jc, i=i:
                                  e.transpose(
                                      ps[0][:, i * 128:(i + 1) * 128],
                                      wf[:, kc * 1024 + jc * 128:
                                         kc * 1024 + (jc + 1) * 128],
                                      identf32[:]),
                                  inc="t" if i == 3 else None)
                    p.wait("vector", "t", ct)
                    cvt = p.op("vector", lambda e, jc=jc, g0=g0: e.tensor_scalar(
                        chs(xTh, jc, g0 * 128, (g0 + 4) * 128),
                        ps[0][:], sval[:], None, ALU.mult), inc="v")
                    state["v"] = cvt
            xt_cur, xt_nxt = xTh, xT2
            ev = [None, None]
            sched = NS_SCHED if NS_TRUNC[0] is None else NS_SCHED[:NS_TRUNC[0]]
            for it, (ca, cb, cc) in enumerate(sched):
                p.wait("tensor", "v", state["v"])
                for ic in range(8):
                    for fb in range(2):
                        pb = ps[fb]
                        p.wait("tensor", "s", ev[fb])
                        cm = None
                        for kc in range(8):
                            cm = p.op("tensor", lambda e, kc=kc, ic=ic, fb=fb,
                                      pb=pb: e.matmul(
                                          pb[:],
                                          chs(xh, kc, ic * 128, (ic + 1) * 128),
                                          chs(xh, kc, fb * 512, (fb + 1) * 512),
                                          start=(kc == 0), stop=(kc == 7)),
                                      inc="t" if kc == 7 else None)
                        p.wait("scalar", "t", cm)
                        ev[fb] = p.op("scalar", lambda e, ic=ic, fb=fb, pb=pb:
                                      e.activation(
                                          chs(abuf, ic, fb * 512,
                                              (fb + 1) * 512),
                                          pb[:], AF.Copy), inc="s")
                if cc != 0.0:
                    cvB = None
                    for ic in range(8):
                        for fb in range(2):
                            pb = ps[fb]
                            p.wait("tensor", "v", cvB)
                            p.wait("tensor", "s", ev[fb])
                            cm = None
                            for mc in range(8):
                                cm = p.op("tensor", lambda e, mc=mc, ic=ic,
                                          fb=fb, pb=pb: e.matmul(
                                              pb[:],
                                              chs(abuf, mc, ic * 128,
                                                  (ic + 1) * 128),
                                              chs(abuf, mc, fb * 512,
                                                  (fb + 1) * 512),
                                              start=(mc == 0), stop=(mc == 7)),
                                          inc="t" if mc == 7 else None)
                            p.wait("vector", "t", cm)
                            p.wait("vector", "s", ev[fb])
                            cvB = p.op("vector", lambda e, ic=ic, fb=fb,
                                       cb=cb: e.tensor_scalar(
                                           chs(bbuf, ic, fb * 512,
                                               (fb + 1) * 512),
                                           chs(abuf, ic, fb * 512,
                                               (fb + 1) * 512),
                                           cb, None, ALU.mult), inc="v")
                            p.wait("vector", "v", cvB)
                            cvB = p.op("vector", lambda e, ic=ic, fb=fb, pb=pb,
                                       cc=cc: e.scalar_tensor_tensor(
                                           chs(bbuf, ic, fb * 512,
                                               (fb + 1) * 512),
                                           pb[:], cc,
                                           chs(bbuf, ic, fb * 512,
                                               (fb + 1) * 512),
                                           ALU.mult, ALU.add), inc="v")
                            ev[fb] = None
                    bwait = cvB
                else:
                    p.wait("vector", "s", ev[0])
                    p.wait("vector", "s", ev[1])
                    cvB = None
                    for ic in range(8):
                        cvB = p.op("vector", lambda e, ic=ic, cb=cb:
                                   e.tensor_scalar(
                                       chs(bbuf, ic, 0, 1024),
                                       chs(abuf, ic, 0, 1024),
                                       cb, None, ALU.mult),
                                   inc="v" if ic == 7 else None)
                    bwait = cvB
                cup = None
                for kc in range(8):
                    for fb in range(2):
                        pb = ps[fb]
                        p.wait("tensor", "v", bwait)
                        p.wait("tensor", "s", ev[fb])
                        p.wait("tensor", "v", cup)
                        cm = None
                        for ic in range(8):
                            cm = p.op("tensor", lambda e, ic=ic, kc=kc, fb=fb,
                                      pb=pb, xt_cur=xt_cur: e.matmul(
                                          pb[:],
                                          chs(xt_cur, ic, kc * 128,
                                              (kc + 1) * 128),
                                          chs(bbuf, ic, fb * 512,
                                              (fb + 1) * 512),
                                          start=(ic == 0), stop=(ic == 7)),
                                      inc="t" if ic == 7 else None)
                        p.wait("vector", "t", cm)
                        cup = p.op("vector", lambda e, kc=kc, fb=fb, pb=pb,
                                   ca=ca: e.scalar_tensor_tensor(
                                       chs(xh, kc, fb * 512, (fb + 1) * 512),
                                       chs(xh, kc, fb * 512, (fb + 1) * 512),
                                       ca, pb[:], ALU.mult, ALU.add), inc="v")
                        ev[fb] = None
                for jc in range(8):
                    for fb in range(2):
                        pb = ps[fb]
                        p.wait("tensor", "v", cup)
                        cm = None
                        for ic in range(8):
                            cm = p.op("tensor", lambda e, ic=ic, jc=jc, fb=fb,
                                      pb=pb, xt_cur=xt_cur: e.matmul(
                                          pb[:],
                                          chs(bbuf, ic, jc * 128,
                                              (jc + 1) * 128),
                                          chs(xt_cur, ic, fb * 512,
                                              (fb + 1) * 512),
                                          start=(ic == 0), stop=(ic == 7)),
                                      inc="t" if ic == 7 else None)
                        p.wait("vector", "t", cm)
                        cup = p.op("vector", lambda e, jc=jc, fb=fb, pb=pb,
                                   ca=ca, xt_cur=xt_cur, xt_nxt=xt_nxt:
                                   e.scalar_tensor_tensor(
                                       chs(xt_nxt, jc, fb * 512,
                                           (fb + 1) * 512),
                                       chs(xt_cur, jc, fb * 512,
                                           (fb + 1) * 512),
                                       ca, pb[:], ALU.mult, ALU.add), inc="v")
                state["v"] = cup
                xt_cur, xt_nxt = xt_nxt, xt_cur
            cpo = None
            p.wait("vector", "v", state["v"])
            for jc in range(8):
                cpo = p.op("vector", lambda e, jc=jc, xt_cur=xt_cur:
                           e.tensor_copy(pT_out[:, jc, :],
                                         chs(xt_cur, jc, 0, 1024)),
                           inc="v" if jc == 7 else None)
            state["v"] = cpo
            state["cx0"] = cpo
            return cpo

        cp1 = ns_polar(Wp1, p1T)
        cp2 = ns_polar(Wp2, p2T)
        if debug:
            p.wait("sync", "v", cp1)
            p.wait("sync", "v", cp2)
            p.dma("sync", dbg_p1, p1T.ap())
            p.dma("sync", dbg_p2, p2T.ap())

        wstage_guard = {"cv": None}

        def load_wT(wdram, dst):
            for c_d in range(8):
                p.wait("gpsimd", "v", wstage_guard["cv"])
                cd = p.dma("gpsimd", wstage.ap(),
                           wdram[c_d * 128:(c_d + 1) * 128, :]
                           .rearrange("p (c n) -> p c n", n=128))
                cvw = tr.run(lambda c, c_d=c_d: wstage[:, c, :], 8,
                             lambda g0, gn, c_d=c_d: dst[:, g0:g0 + gn,
                                                         c_d * 128:(c_d + 1) * 128]
                             .rearrange("p c n -> p c n"),
                             extra_waits=[("dgp", cd)])
                wstage_guard["cv"] = cvw
            return cvw

        cwq = load_wT(Wq, wqT)
        cwv = load_wT(Wv, wvT)

        cid = p.dma("gpsimd", qbuf[:, :, 0:1],
                    idv.rearrange("(c p) -> p c", p=128).unsqueeze(2),
                    allow_slow_non_contiguous=True)

        lf32 = lflat.bitcast(F32)
        lnst = LNState(stack, nc, "l2a", scr_ext=lf32[:, 4096:5120],
                       xc_ext=lf32[:, 5120:6144])
        xt = [lf32[:, 0:1024], lf32[:, 1024:2048]]
        xn = lflat[:, 12288:13312]
        yt = [lf32[:, 2048:3072], lf32[:, 3072:4096]]
        cv_prev = None
        ln_prev = None
        ev_q = None
        ev_v = None
        vdma = None
        p.wait("sync", "v", cp2)
        for t in range(NT):
            b = t % 2
            if t >= 2:
                p.wait("sync", "v", cv_prev)
            cx = p.dma("sync", xt[b][:], x_in[t * 128:(t + 1) * 128, :])
            p.wait("vector", "dsync", cx)
            p.wait("vector", "s", ln_prev)
            p.wait("scalar", "v", cv_prev)
            cs = emit_layernorm(p, lnst, xt[b][:], [xn[:]], consts)
            cv_prev = tr.run(lambda c: xn[:, c * 128:(c + 1) * 128], 8,
                             lambda g0, gn: xnT_t[:, g0:g0 + gn, :],
                             extra_waits=[("s", cs), ("v", cwq), ("v", cwv),
                                          ("v", cp2)])
            for eo in range(8):
                pb = ps[eo % 2]
                p.wait("tensor", "v", cv_prev)
                p.wait("tensor", "s", ev_q)
                cm = None
                for c in range(8):
                    cm = p.op("tensor", lambda e, c=c, eo=eo, pb=pb: e.matmul(
                        pb[:, 0:128], wqT[:, c, eo * 128:(eo + 1) * 128],
                        xnT_t[:, c, :], start=(c == 0), stop=(c == 7)),
                        inc="t" if c == 7 else None)
                p.wait("scalar", "t", cm)
                ev_q = p.op("scalar", lambda e, eo=eo, t=t, pb=pb: e.activation(
                    qbuf[:, eo, 1 + t * 128:1 + (t + 1) * 128], pb[:, 0:128],
                    AF.Copy), inc="s")
            p.wait("vector", "dgp", vdma)
            for eo in (() if L2_NO_V[0] else range(8)):
                pb = ps[eo % 2]
                p.wait("tensor", "s", ev_q)
                p.wait("tensor", "v", ev_v)
                cm = None
                for c in range(8):
                    cm = p.op("tensor", lambda e, c=c, eo=eo, pb=pb: e.matmul(
                        pb[:, 0:128], wvT[:, c, eo * 128:(eo + 1) * 128],
                        xnT_t[:, c, :], start=(c == 0), stop=(c == 7)),
                        inc="t" if c == 7 else None)
                p.wait("vector", "t", cm)
                ev_v = p.op("vector", lambda e, eo=eo, pb=pb: e.tensor_copy(
                    vt_t[:, eo, :], pb[:, 0:128]), inc="v")
            if not L2_NO_V[0]:
                p.wait("gpsimd", "v", ev_v)
                vdma = p.dma("gpsimd", v_scr[:, :, t * 128:(t + 1) * 128],
                             vt_t.ap())
            if debug and t == 0:
                p.wait("sync", "s", ev_q)
                p.dma("sync", dbg_t0, qbuf[:, :, 1:129])
        p.wait("scalar", "dgp", cid)

        zn_v = [None]
        QOFF = {"Q": 0, "L": 0}

        def arr_ap(buf, c, start, count, step):
            base = qbuf if buf == "Q" else lbuf
            return base[:, c, start:start + (count - 1) * step + 1:step]

        def combine(dst, Lsrc, Rsrc, n):
            dbuf, dstart, dstep = dst
            lb, ls, lst = Lsrc
            rb, rs_, rst = Rsrc
            for b0 in range(0, n, 256):
                bn = min(256, n - b0)
                p.wait("tensor", "v", zn_v[0])
                cm = None
                for eo in range(8):
                    for c in range(8):
                        cm = p.op("tensor", lambda e, eo=eo, c=c, b0=b0, bn=bn,
                                  lb=lb, ls=ls, lst=lst: e.matmul(
                                      zps[:, eo, 0:bn],
                                      p1T[:, c, eo * 128:(eo + 1) * 128],
                                      arr_ap(lb, c, ls + b0 * lst, bn, lst),
                                      start=(c == 0), stop=False), inc=None)
                    for c in range(8):
                        cm = p.op("tensor", lambda e, eo=eo, c=c, b0=b0, bn=bn,
                                  rb=rb, rs_=rs_, rst=rst: e.matmul(
                                      zps[:, eo, 0:bn],
                                      p2T[:, c, eo * 128:(eo + 1) * 128],
                                      arr_ap(rb, c, rs_ + b0 * rst, bn, rst),
                                      start=False, stop=(c == 7)),
                                  inc="t" if (eo == 7 and c == 7) else None)
                p.wait("scalar", "t", cm)
                csq = p.op("scalar", lambda e, bn=bn: e.activation(
                    sq_bf[:, 0:8 * 256].rearrange("p (c n) -> p c n", n=256)
                    [:, :, 0:bn] if bn != 256 else sq_bf[:].rearrange(
                        "p (c n) -> p c n", n=256),
                    zps[:, :, 0:bn] if bn != 256 else zps[:],
                    AF.Square), inc="s")
                p.wait("tensor", "s", csq)
                cm2 = None
                for c in range(8):
                    cm2 = p.op("tensor", lambda e, c=c, bn=bn: e.matmul(
                        pr1[0:1, 0:bn], onescol[:],
                        sq_bf[:, c * 256:c * 256 + bn],
                        start=(c == 0), stop=(c == 7)),
                        inc="t" if c == 7 else None)
                p.wait("scalar", "t", cm2)
                ceps = consts.get(1e-8, "scalar")
                cs1 = p.op("scalar", lambda e, bn=bn: e.activation(
                    lnrow[0:1, 0:bn], pr1[0:1, 0:bn], AF.Ln,
                    scale=1.0 / 1024.0, bias=ceps[0:1, :]), inc="s")
                p.wait("scalar", "s", cs1)
                cs2 = p.op("scalar", lambda e, bn=bn: e.activation(
                    rstd_row[0:1, 0:bn], lnrow[0:1, 0:bn], AF.Exp,
                    scale=-0.5), inc="s")
                p.wait("tensor", "s", cs2)
                cmb = p.op("tensor", lambda e, bn=bn: e.matmul(
                    ps[0][:, 0:bn], ones1[:], rstd_row[0:1, 0:bn],
                    start=True, stop=True), inc="t")
                p.wait("vector", "t", cmb)
                cvr = p.op("vector", lambda e, bn=bn: e.tensor_copy(
                    rstdb[:, 0:bn], ps[0][:, 0:bn]), inc="v")
                p.wait("vector", "v", cvr)
                czn = None
                for eo in range(8):
                    czn = p.op("vector", lambda e, eo=eo, b0=b0, bn=bn,
                               dbuf=dbuf, dstart=dstart, dstep=dstep:
                               e.tensor_tensor(
                                   arr_ap(dbuf, eo, dstart + b0 * dstep, bn,
                                          dstep),
                                   zps[:, eo, 0:bn], rstdb[:, 0:bn],
                                   ALU.mult),
                               inc="v" if eo == 7 else None)
                zn_v[0] = czn
            return zn_v[0]

        lvl_cursor = [0]

        def emit_scan(buf, start, n, depth=0):
            if n < 2:
                return
            if SCAN_DEPTH[0] is not None and depth >= SCAN_DEPTH[0]:
                return
            rn = n // 2
            red_start = lvl_cursor[0]
            lvl_cursor[0] += rn
            combine(("L", red_start, 1), (buf, start, 2), (buf, start + 1, 2),
                    rn)
            emit_scan("L", red_start, rn, depth + 1)
            if SCAN_DEPTH[0] is not None and depth + 1 >= SCAN_DEPTH[0]:
                return
            ne = (n - 2) // 2 if n % 2 == 0 else (n - 1) // 2
            if ne > 0:
                combine((buf, start + 2, 2), ("L", red_start, 1),
                        (buf, start + 2, 2), ne)
            for b0 in range(0, rn, 1024):
                bn = min(1024, rn - b0)
                czo = None
                for c in range(8):
                    czo = p.op("vector", lambda e, c=c, b0=b0, bn=bn,
                               buf=buf, start=start, red_start=red_start:
                               e.tensor_copy(
                                   arr_ap(buf, c, start + 1 + 2 * b0, bn, 2),
                                   arr_ap("L", c, red_start + b0, bn, 1)),
                               inc="v" if c == 7 else None)
                zn_v[0] = max(zn_v[0] or 0, czo)
            lvl_cursor[0] -= 0

        p.wait("vector", "dgp", cid)
        p.wait("tensor", "dgp", cid)
        p.wait("tensor", "s", ev_q)
        p.wait("tensor", "v", ev_v)
        emit_scan("Q", 0, 4097)

        if debug:
            p.wait("sync", "v", p.counts["v"])
            p.wait("sync", "s", p.counts["s"])
            p.wait("sync", "dgp", cid)
            p.dma("sync", dbg_q, qbuf.ap())
            p.dma("sync", dbg_l, lbuf.ap())
            p.dma("sync", dbg_w, wqT.ap())
        p.wait("gpsimd", "t", p.counts["t"])
        cwc = load_wT(Wc, wqT)
        out_dma = [None] * NT
        ev_f = None
        vload = None
        qsv_used = None
        cqs_prev = None
        p.wait("sync", "v", p.counts["v"])
        p.wait("sync", "s", p.counts["s"])
        for t in range(NT):
            b = t % 2
            if t >= 2:
                p.wait("sync", "v", ev_f)
                p.wait("sync", "dsync", out_dma[t - 2])
            cx = p.dma("sync", xt[b][:], x_in[t * 128:(t + 1) * 128, :])
            p.wait("scalar", "dgp", vdma)
            p.wait("scalar", "v", cqs_prev)
            vload = p.dma("scalar", vt_t.ap(), v_scr[:, :, t * 128:(t + 1) * 128])
            p.wait("vector", "dscalar", vload)
            p.wait("vector", "t", qsv_used)
            cqs = p.op("vector", lambda e, t=t: e.tensor_tensor(
                qsv_t.ap(), qbuf[:, :, 1 + t * 128:1 + (t + 1) * 128],
                vt_t.ap(), ALU.mult), inc="v")
            cqs_prev = cqs
            p.wait("tensor", "v", cqs)
            p.wait("tensor", "v", cwc)
            if t >= 1:
                p.wait("tensor", "v", ev_f)
            cm2 = None
            for fb in range(2):
                for c in range(8):
                    cm2 = p.op("tensor", lambda e, c=c, fb=fb: e.matmul(
                        ps[fb][:], qsv_t[:, c, :],
                        wqT[:, c, fb * 512:(fb + 1) * 512],
                        start=(c == 0), stop=(c == 7)),
                        inc="t" if c == 7 else None)
            qsv_used = cm2
            p.wait("vector", "t", cm2)
            p.wait("vector", "dsync", cx)
            p.op("vector", lambda e, b=b: e.tensor_add(
                yt[b][:, 0:512], ps[0][:], xt[b][:, 0:512]))
            ev_f = p.op("vector", lambda e, b=b: e.tensor_add(
                yt[b][:, 512:1024], ps[1][:], xt[b][:, 512:1024]), inc="v")
            p.wait("sync", "v", ev_f)
            out_dma[t] = p.dma("sync", x_out[t * 128:(t + 1) * 128, :], yt[b][:])
        p.wait("sync", "dsync", p.counts["dsync"])
        p.wait("sync", "dscalar", p.counts["dscalar"])
        if fused is None:
            p.emit()
    return decls



def build_l3ffn(nc, fused=None, io=None, pfx=""):
    decls = {}
    decl = stage_decl(nc, io, pfx, decls)
    x_in = decl("x_in", [T_SEQ, D], "ExternalInput", F32)
    Wqkv = decl("Wqkv", [6, D, 3072], "ExternalInput", F32)
    q_scale = decl("q_scale", [6, 64], "ExternalInput", F32)
    k_scale = decl("k_scale", [6, 64], "ExternalInput", F32)
    Wo = decl("Wo", [6, D, D], "ExternalInput", F32)
    ln_g = decl("ln_g", [6, D], "ExternalInput", F32)
    ln_b = decl("ln_b", [6, D], "ExternalInput", F32)
    W1 = decl("W1", [6, D, 5460], "ExternalInput", F32)
    b1 = decl("b1", [6, 5460], "ExternalInput", F32)
    W2 = decl("W2", [6, 2730, D], "ExternalInput", F32)
    b2 = decl("b2", [6, D], "ExternalInput", F32)
    x_out = decl("x_out", [T_SEQ, D], "ExternalOutput", F32)
    zA = nc.dram_tensor(pfx + "zA", [T_SEQ, D], F32, kind="Internal").ap()
    zB = nc.dram_tensor(pfx + "zB", [T_SEQ, D], F32, kind="Internal").ap()
    xnT_scr = nc.dram_tensor(pfx + "xnT_scr", [128, 8, T_SEQ], BF16, kind="Internal").ap()

    HCH = [(0, 512), (512, 512), (1024, 512), (1536, 512), (2048, 512),
           (2560, 256)]

    with contextlib.ExitStack() as stack:
        _PFX[0] = pfx
        if fused is None:
            p = Prog(nc, stack)
        else:
            p = fused
            p.barrier()
        ident = alloc(stack, nc, "ident", [128, 128], BF16)
        onesr = alloc(stack, nc, "onesr", [1, 512], BF16)
        cones = make_identity(p, ident)
        p.op("gpsimd", lambda e: e.memset(onesr[:], 1.0))
        consts = Consts(stack, nc, p)
        tr = Transposer(stack, nc, p, ident)
        fc = FlashCtx(stack, nc, p, tr, consts, max_klen=4096)
        ln = LNState(stack, nc, "l3f", scr_ext=fc.S[:, 0:D])
        xt = [fc.S[:, 1024:2048], fc.S[:, 2048:3072]]
        xn = fc.attn[:, 1024:2048]
        yt = [fc.S[:, 3072:4096], fc.attn[:, 2048:4096].bitcast(F32)]
        tabs = alloc(stack, nc, "tabs", [128, NT, 4, 64], BF16)
        zsT = alloc(stack, nc, "zsT", [128, 8, 896], BF16)
        qkvf = alloc(stack, nc, "qkvf", [128, 3072], F32)
        rtmp = alloc(stack, nc, "rtmp", [128, 1024], F32)
        rnrm = alloc(stack, nc, "rnrm", [128, 16], F32)
        qb_t = alloc(stack, nc, "qb_t", [128, 1024], BF16)
        kb_t = alloc(stack, nc, "kb_t", [128, 1024], BF16)
        qs8 = alloc(stack, nc, "qs8", [128, 64], F32)
        ks1 = alloc(stack, nc, "ks1", [128, 64], F32)
        qT_blk = alloc(stack, nc, "qT_blk", [128, 8, 512], BF16)
        kT_blk = alloc(stack, nc, "kT_blk", [128, 8, 768], BF16)
        v_blk = alloc(stack, nc, "v_blk", [128, 6, 1024], BF16)
        y_sb = alloc(stack, nc, "y_sb", [128, 4, D], BF16)
        wqkvb = alloc(stack, nc, "wqkvb", [128, 8, 3072], BF16)
        wob = alloc(stack, nc, "wob", [128, 8, D], BF16)
        b1r = alloc(stack, nc, "b1r", [1, 5460], BF16)
        b2r = alloc(stack, nc, "b2r", [1, D], BF16)
        gbrow = alloc(stack, nc, "gbrow", [1, 2 * D], BF16)
        gtile = qkvf[:, 2048:3072]
        btile = zsT.ap().rearrange("p c m -> p (c m)")[:, 4096:6144].bitcast(F32)
        sctile = alloc(stack, nc, "sctile", [1, 128], BF16)
        agb = alloc(stack, nc, "agb", [128, 512], BF16)
        jr = alloc(stack, nc, "jr", [1, 32], F32)
        thr = alloc(stack, nc, "thr", [1, 32], F32)
        lnb = alloc(stack, nc, "lnb", [1, 32], F32)
        trow = alloc(stack, nc, "trow", [1, 128], F32)
        ftile = alloc(stack, nc, "ftile", [128, 160], F32)
        fint = alloc(stack, nc, "fint", [128, 32], mybir.dt.int32)
        hT = v_blk.ap().rearrange("p c m -> p (c m)")[:, 4096:4608]
        xnt_in = [qb_t[:, 0:1024].rearrange("p (c n) -> p c n", n=128),
                  kb_t[:, 0:1024].rearrange("p (c n) -> p c n", n=128)]
        ps_a = alloc_psum(stack, nc, "ps_a", [128, 512], F32)
        ps_g = alloc_psum(stack, nc, "ps_g", [128, 512], F32)
        ps_y = [alloc_psum(stack, nc, f"ps_yl{i}", [128, 512], F32)
                for i in range(2)]
        p.wait("tensor", "g", cones)

        def accv(t):
            if t < 24:
                return wqkvb.ap().rearrange("p c m -> p (c m)")[
                    :, t * 1024:(t + 1) * 1024]
            return wob.ap().rearrange("p c m -> p (c m)")[
                :, (t - 24) * 1024:(t - 23) * 1024]
        w1a_b = [qT_blk.ap().rearrange("p c m -> p (c m)"),
                 kT_blk.ap().rearrange("p c m -> p (c m)")[:, 0:4096]]
        w1g_b = [v_blk.ap().rearrange("p c m -> p (c m)")[:, 0:4096],
                 y_sb.ap().rearrange("p c m -> p (c m)")[:, 0:4096]]
        w2c_b = [qkvf[:, 0:2048].bitcast(BF16),
                 zsT.ap().rearrange("p c m -> p (c m)")[:, 0:4096]]
        w1a = [b.rearrange("p (c m) -> p c m", m=512) for b in w1a_b]
        w1g = [b.rearrange("p (c m) -> p c m", m=512) for b in w1g_b]
        w2c = [b.rearrange("p (c m) -> p c m", m=1024) for b in w2c_b]

        cj = p.op("gpsimd", lambda e: e.iota(
            jr[:], [[1, 32]], base=0, channel_multiplier=0,
            allow_small_or_imprecise_dtypes=True), inc="g")
        p.wait("scalar", "g", cj)
        cl2pi = consts.get(-math.log(2 * math.pi), "scalar")
        cth = p.op("scalar", lambda e: e.activation(
            thr[:], jr[:], AF.Exp, scale=float(-math.log(1e4) / 32.0),
            bias=cl2pi[0:1, :]), inc="s")
        p.wait("vector", "g", cj)
        cv0 = p.op("vector", lambda e: e.tensor_scalar(
            lnb[:], jr[:], 2.0 / 89.6, 25.6 / 89.6, ALU.mult, ALU.add), inc="v")
        p.wait("scalar", "v", cv0)
        clb = p.op("scalar", lambda e: e.activation(
            lnb[:], lnb[:], AF.Ln), inc="s")
        p.wait("vector", "s", clb)
        clb2 = p.op("vector", lambda e: e.tensor_scalar(
            lnb[:], lnb[:], 1.0 / 128.0, None, ALU.mult), inc="v")
        st = {"tu": None, "av": None}
        for t in range(NT):
            p.wait("gpsimd", "t", st["tu"])
            cg = p.op("gpsimd", lambda e, t=t: e.iota(
                trow[:], [[1, 128]], base=t * 128, channel_multiplier=0,
                allow_small_or_imprecise_dtypes=True), inc="g")
            p.wait("tensor", "g", cg)
            p.wait("tensor", "s", cth)
            p.wait("tensor", "v", clb2)
            p.wait("tensor", "v", st["av"])
            cm = p.op("tensor", lambda e: e.matmul(
                ps_a[:, 0:32], trow[:], thr[:], start=True, stop=True), inc="t")
            p.wait("vector", "t", cm)
            for which, off, dlo in (("sin", 0.0, 64), ("cos", 0.25, 32)):
                if off:
                    c0 = p.op("vector", lambda e, off=off: e.tensor_scalar(
                        ftile[:, 0:32], ps_a[:, 0:32], off, None, ALU.add),
                        inc="v")
                else:
                    c0 = p.op("vector", lambda e: e.tensor_copy(
                        ftile[:, 0:32], ps_a[:, 0:32]), inc="v")
                p.wait("vector", "v", c0)
                c1 = p.op("vector", lambda e: e.tensor_copy(
                    fint[:], ftile[:, 0:32]), inc="v")
                p.wait("vector", "v", c1)
                c2 = p.op("vector", lambda e: e.tensor_copy(
                    rtmp[:, 0:32], fint[:]), inc="v")
                p.wait("vector", "v", c2)
                c3 = p.op("vector", lambda e: e.tensor_sub(
                    ftile[:, 0:32], ftile[:, 0:32], rtmp[:, 0:32]), inc="v")
                p.wait("scalar", "v", c3)
                cs = p.op("scalar", lambda e, dlo=dlo: e.activation(
                    ftile[:, dlo:dlo + 32], ftile[:, 0:32],
                    AF.Sin, scale=float(2 * math.pi)), inc="s")
                p.wait("vector", "s", cs)
            p.wait("gpsimd", "t", cm)
            cg2 = p.op("gpsimd", lambda e, t=t: e.iota(
                trow[:], [[1, 128]], base=t * 128 - 2048, channel_multiplier=0,
                allow_small_or_imprecise_dtypes=True), inc="g")
            p.wait("tensor", "g", cg2)
            cm2 = p.op("tensor", lambda e: e.matmul(
                ps_a[:, 32:64], trow[:], lnb[:], start=True, stop=True), inc="t")
            st["tu"] = cm2
            p.wait("scalar", "t", cm2)
            cs1 = p.op("scalar", lambda e: e.activation(
                ftile[:, 96:128], ps_a[:, 32:64], AF.Exp), inc="s")
            cs2 = p.op("scalar", lambda e: e.activation(
                ftile[:, 128:160], ps_a[:, 32:64], AF.Exp, scale=-1.0), inc="s")
            p.wait("vector", "s", cs2)
            cv = None
            for comp, (tlo, slo) in enumerate(((32, 96), (64, 96),
                                               (32, 128), (64, 128))):
                for half in range(2):
                    cv = p.op("vector", lambda e, t=t, comp=comp, tlo=tlo,
                              slo=slo, half=half: e.tensor_tensor(
                                  tabs[:, t, comp, half * 32:(half + 1) * 32],
                                  ftile[:, tlo:tlo + 32],
                                  ftile[:, slo:slo + 32], ALU.mult),
                              inc="v" if comp == 3 and half == 1 else None)
            st["av"] = cv

        ev_f = None
        out_dma = [None] * NT
        lnp = None
        for t in range(NT):
            b = t % 2
            if t >= 2:
                p.wait("sync", "v", p.counts["v"] if t == 2 else None)
                p.wait("sync", "s", lnp)
                p.wait("scalar", "dsync", out_dma[t - 2])
            cx = p.dma("sync", xt[b][:], x_in[t * 128:(t + 1) * 128, :])
            p.wait("vector", "dsync", cx)
            p.wait("vector", "s", lnp)
            cs = emit_layernorm(p, ln, xt[b][:], [yt[b][:]], consts)
            lnp = cs
            p.wait("sync", "s", cs)
            out_dma[t] = p.dma("sync", zA[t * 128:(t + 1) * 128, :], yt[b][:])
        zin, zout = zA, zB

        for li in range(6):
            p.wait("gpsimd", "t", p.counts["t"])
            p.wait("gpsimd", "v", p.counts["v"])
            cw1 = p.dma("gpsimd", wqkvb.ap(),
                        Wqkv[li].rearrange("(c p) e -> p c e", p=128))
            cw2 = p.dma("gpsimd", wob.ap(),
                        Wo[li].rearrange("(c p) e -> p c e", p=128))
            cqs = p.dma("gpsimd", sctile[0:1, 0:64], q_scale[li].unsqueeze(0))
            cks = p.dma("gpsimd", sctile[0:1, 64:128], k_scale[li].unsqueeze(0))
            p.wait("tensor", "dgp", cks)
            p.wait("tensor", "v", ev_f)
            cmq = p.op("tensor", lambda e: e.matmul(
                ps_a[:, 0:64], onesr[0:1, 0:128], sctile[0:1, 0:64],
                start=True, stop=True), inc="t")
            p.wait("vector", "t", cmq)
            cv1 = p.op("vector", lambda e: e.tensor_scalar(
                qs8[:], ps_a[:, 0:64], 8.0, None, ALU.mult), inc="v")
            p.wait("tensor", "v", cv1)
            cmk = p.op("tensor", lambda e: e.matmul(
                ps_a[:, 0:64], onesr[0:1, 0:128], sctile[0:1, 64:128],
                start=True, stop=True), inc="t")
            p.wait("vector", "t", cmk)
            cv2 = p.op("vector", lambda e: e.tensor_copy(ks1[:], ps_a[:, 0:64]),
                       inc="v")
            p.wait("tensor", "v", cv2)
            p.wait("sync", "v", p.counts["v"])

            ev_prev = None
            ev_par = [None, None]
            blk_gate = p.counts["v"]
            last_fin = None
            ztile_v = None
            for qb in range(8):
                qt0 = qb * 4
                kt0 = max(0, qt0 - 2)
                ktiles = list(range(kt0, qt0 + 4))
                slab = [kt0 - 1] + ktiles
                cv_prev = None
                for si, bt in enumerate(slab):
                    if bt < 0:
                        p.wait("gpsimd", "t", p.counts["t"])
                        czg = p.op("gpsimd", lambda e: e.memset(
                            zsT[:, :, 0:128], 0.0), inc="g")
                        p.wait("tensor", "g", czg)
                        continue
                    b = bt % 2
                    p.wait("sync", "v", ztile_v)
                    cx = p.dma("sync", xt[b][:], zin[bt * 128:(bt + 1) * 128, :])
                    p.wait("vector", "dsync", cx)
                    p.wait("vector", "v", cv_prev)
                    cb = p.op("vector", lambda e, b=b: e.tensor_copy(
                        xn[:], xt[b][:]), inc="v")
                    ztile_v = cb
                    cv_prev = tr.run(lambda c: xn[:, c * 128:(c + 1) * 128], 8,
                                     lambda g0, gn, si=si: zsT[:, g0:g0 + gn,
                                                                si * 128:
                                                                (si + 1) * 128],
                                     extra_waits=[("v", cb)])
                p.wait("tensor", "v", cv_prev)
                blk_ready = None
                evv_prev = None
                for kt in ktiles:
                    slot = kt - kt0
                    sbase = (slot + 1) * 128
                    is_q = kt >= qt0
                    for grp in range(6):
                        pb = [ps_a, ps_g][grp % 2]
                        p.wait("tensor", "s", ev_par[grp % 2])
                        cm = None
                        for c in range(8):
                            off = sbase - (0 if c < 4 else 1)
                            cm = p.op("tensor", lambda e, c=c, grp=grp, off=off,
                                      pb=pb: e.matmul(
                                          pb[:],
                                          zsT[:, c, off:off + 128],
                                          wqkvb[:, c, grp * 512:(grp + 1) * 512],
                                          start=(c == 0), stop=(c == 7)),
                                      inc="t" if c == 7 else None)
                        p.wait("scalar", "t", cm)
                        p.wait("scalar", "v", blk_gate)
                        ev_prev = p.op("scalar", lambda e, grp=grp, pb=pb:
                                       e.activation(
                                           qkvf[:, grp * 512:(grp + 1) * 512],
                                           pb[:], AF.Copy), inc="s")
                        ev_par[grp % 2] = ev_prev
                    p.wait("vector", "s", ev_prev)

                    def l2n_rope(src_off, sc_tile, tab_cos, tab_sin, dstb):
                        sv = qkvf[:, src_off:src_off + 1024].rearrange(
                            "p (h d) -> p h d", d=64)
                        tv = rtmp[:].rearrange("p (h d) -> p h d", d=64)
                        c1 = p.op("vector", lambda e, src_off=src_off:
                                  e.tensor_mul(
                                      rtmp[:], qkvf[:, src_off:src_off + 1024],
                                      qkvf[:, src_off:src_off + 1024]), inc="v")
                        p.wait("vector", "v", c1)
                        c2 = p.op("vector", lambda e, tv=tv: e.tensor_reduce(
                            rnrm[:].rearrange("p (h o) -> p h o", o=1), tv,
                            AX.X, ALU.add), inc="v")
                        p.wait("scalar", "v", c2)
                        c3 = p.op("scalar", lambda e: e.activation(
                            rnrm[:], rnrm[:], AF.Sqrt), inc="s")
                        p.wait("vector", "s", c3)
                        c4 = p.op("vector", lambda e: e.tensor_scalar(
                            rnrm[:], rnrm[:], 1e-12, None, ALU.max), inc="v")
                        p.wait("vector", "v", c4)
                        c5 = p.op("vector", lambda e: e.reciprocal(
                            rnrm[:], rnrm[:]), inc="v")
                        p.wait("vector", "v", c5)
                        c6 = p.op("vector", lambda e, sv=sv, tv=tv:
                                  e.tensor_tensor(
                                      tv, sv,
                                      rnrm[:].rearrange("p (h o) -> p h o", o=1)
                                      .broadcast_to([128, 16, 64]), ALU.mult),
                                  inc="v")
                        p.wait("vector", "v", c6)
                        c7 = p.op("vector", lambda e, tv=tv, sc_tile=sc_tile:
                                  e.tensor_tensor(
                                      tv, tv, sc_tile.unsqueeze(1)
                                      .broadcast_to([128, 16, 64]), ALU.mult),
                                  inc="v")
                        p.wait("vector", "v", c7)
                        E, O = tv[:, :, 0:32], tv[:, :, 32:64]
                        dv = dstb[:].rearrange("p (h d) -> p h d", d=64)
                        dE, dO = dv[:, :, 0:32], dv[:, :, 32:64]
                        xc32 = ln.xc[:, 0:512].rearrange("p (h d) -> p h d",
                                                         d=32)
                        bc = lambda ap: ap.unsqueeze(1).broadcast_to(
                            [128, 16, 32])
                        cE, cO = bc(tab_cos[0:128, 0:32]), bc(tab_cos[0:128, 32:64])
                        sE, sO = bc(tab_sin[0:128, 0:32]), bc(tab_sin[0:128, 32:64])
                        p.op("vector", lambda e, dE=dE, E=E, cE=cE:
                             e.tensor_tensor(dE, E, cE, ALU.mult))
                        c8 = p.op("vector", lambda e, dO=dO, O=O, sE=sE:
                                  e.tensor_tensor(dO, O, sE, ALU.mult), inc="v")
                        p.wait("vector", "v", c8)
                        c9 = p.op("vector", lambda e, dE=dE, dO=dO:
                                  e.tensor_sub(dE, dE, dO), inc="v")
                        p.wait("vector", "v", c9)
                        p.op("vector", lambda e, dO=dO, O=O, cO=cO:
                             e.tensor_tensor(dO, O, cO, ALU.mult))
                        ca = p.op("vector", lambda e, xc32=xc32, E=E, sO=sO:
                                  e.tensor_tensor(xc32, E, sO, ALU.mult),
                                  inc="v")
                        p.wait("vector", "v", ca)
                        cb2 = p.op("vector", lambda e, dO=dO, xc32=xc32:
                                   e.tensor_tensor(dO, dO, xc32, ALU.add),
                                   inc="v")
                        return cb2
                    cqr = None
                    if is_q:
                        cqr = l2n_rope(0, qs8[:], tabs[:, kt, 0, :],
                                       tabs[:, kt, 1, :], qb_t)
                        p.wait("vector", "v", cqr)
                    ckr = l2n_rope(1024, ks1[:], tabs[:, kt, 2, :],
                                   tabs[:, kt, 3, :], kb_t)
                    cvv = p.op("vector", lambda e, slot=slot: e.tensor_copy(
                        v_blk[:, slot, :], qkvf[:, 2048:3072]), inc="v")
                    evv_prev = cvv
                    if is_q:
                        ql = kt - qt0
                        cvt = tr.run(
                            lambda c: qb_t[:, c * 128:(c + 1) * 128], 8,
                            lambda g0, gn, ql=ql: qT_blk[:, g0:g0 + gn,
                                                         ql * 128:
                                                         (ql + 1) * 128],
                            extra_waits=[("v", cqr)])
                    cvk = tr.run(
                        lambda c: kb_t[:, c * 128:(c + 1) * 128], 8,
                        lambda g0, gn, slot=slot: kT_blk[:, g0:g0 + gn,
                                                         slot * 128:
                                                         (slot + 1) * 128],
                        extra_waits=[("v", ckr)])
                    blk_ready = cvk
                    blk_gate = cvv
                for h in range(16):
                    base = 64 * (h % 2)
                    hp = h // 2
                    for ql in range(4):
                        qi = qt0 + ql
                        klo = max(0, qi - 2)
                        key_tiles = list(range(klo, qi + 1))
                        masks = [((qi - klo) * 128, fc.mask_causal[:])]
                        if qi - 2 >= 0:
                            masks.append((0, fc.mask_up_in[:]))
                        cfin = emit_flash(
                            fc,
                            qT_blk[base:base + 64, hp,
                                   ql * 128:(ql + 1) * 128],
                            lambda g0, ncols, base=base, hp=hp, klo=klo,
                            kt0=kt0: kT_blk[base:base + 64, hp,
                                            (klo - kt0 + g0) * 128:
                                            (klo - kt0 + g0) * 128 + ncols],
                            lambda kt, kt0=kt0, h=h: v_blk[:, kt - kt0,
                                                           h * 64:(h + 1) * 64],
                            y_sb[:, ql, h * 64:(h + 1) * 64],
                            64, key_tiles, masks,
                            extra_tensor_waits=[("s", ev_prev),
                                                ("v", blk_ready),
                                                ("v", evv_prev)])
                        last_fin = cfin
                for ql in range(4):
                    t = qt0 + ql
                    b = t % 2
                    p.wait("sync", "v", last_fin)
                    cx = p.dma("sync", xt[b][:], zin[t * 128:(t + 1) * 128, :])
                    cv = tr.run(lambda c, ql=ql: y_sb[:, ql,
                                                      c * 128:(c + 1) * 128],
                                8, fc.attnT[:], extra_waits=[("v", last_fin)])
                    p.wait("tensor", "v", cv)
                    p.wait("tensor", "v", ev_f)
                    p.wait("tensor", "dgp", cw2)
                    cm2 = None
                    for fb in range(2):
                        for c in range(8):
                            cm2 = p.op("tensor", lambda e, c=c, fb=fb: e.matmul(
                                ps_y[fb][:], fc.attnT[:, c * 128:(c + 1) * 128],
                                wob[:, c, fb * 512:(fb + 1) * 512],
                                start=(c == 0), stop=(c == 7)),
                                inc="t" if c == 7 else None)
                    p.wait("vector", "t", cm2)
                    p.wait("vector", "dsync", cx)
                    p.op("vector", lambda e, b=b: e.tensor_add(
                        yt[b][:, 0:512], ps_y[0][:], xt[b][:, 0:512]))
                    ev_f = p.op("vector", lambda e, b=b: e.tensor_add(
                        yt[b][:, 512:1024], ps_y[1][:], xt[b][:, 512:1024]),
                        inc="v")
                    ztile_v = ev_f
                    p.wait("sync", "v", ev_f)
                    p.dma("sync", zout[t * 128:(t + 1) * 128, :], yt[b][:])
            zin, zout = zout, zin

            cgr = p.dma("gpsimd", gbrow[0:1, 0:D], ln_g[li].unsqueeze(0))
            cbr = p.dma("gpsimd", gbrow[0:1, D:2 * D], ln_b[li].unsqueeze(0))
            cbf1 = p.dma("gpsimd", b1r.ap(), b1[li].unsqueeze(0))
            cbf2 = p.dma("gpsimd", b2r.ap(), b2[li].unsqueeze(0))
            p.wait("tensor", "dgp", cbf2)
            cvg = None
            for which, dst in ((0, gtile), (1, btile)):
                p.wait("tensor", "v", ev_f)
                p.wait("tensor", "v", cvg)
                cmg = p.op("tensor", lambda e, which=which: e.matmul(
                    ps_y[0][:], onesr[0:1, 0:128],
                    gbrow[0:1, which * D:which * D + 512],
                    start=True, stop=True), inc="t")
                cmg2 = p.op("tensor", lambda e, which=which: e.matmul(
                    ps_y[1][:], onesr[0:1, 0:128],
                    gbrow[0:1, which * D + 512:(which + 1) * D],
                    start=True, stop=True), inc="t")
                p.wait("vector", "t", cmg2)
                p.op("vector", lambda e, dst=dst: e.tensor_copy(
                    dst[:, 0:512], ps_y[0][:]))
                cvg = p.op("vector", lambda e, dst=dst: e.tensor_copy(
                    dst[:, 512:1024], ps_y[1][:]), inc="v")
            lnp = None
            xdma = None
            for t in range(NT):
                b = t % 2
                p.wait("sync", "v", ev_f)
                cx1 = p.dma("sync", xt[b][:, 0:512],
                            zin[t * 128:(t + 1) * 128, 0:512])
                if t == 0:
                    cx2 = p.dma("sync", xt[b][1:128, 512:1024],
                                zin[0:127, 512:1024])
                    p.wait("vector", "dsync", cx2)
                    czr = p.op("vector", lambda e, b=b: e.memset(
                        xt[b][0:1, 512:1024], 0.0), inc="v")
                    p.wait("vector", "v", czr)
                else:
                    cx2 = p.dma("sync", xt[b][:, 512:1024],
                                zin[t * 128 - 1:(t + 1) * 128 - 1, 512:1024])
                    p.wait("vector", "dsync", cx2)
                p.wait("vector", "dsync", cx1)
                p.wait("vector", "s", lnp)
                cs = emit_layernorm(p, ln, xt[b][:], [rtmp[:]], consts)
                lnp = cs
                p.wait("vector", "s", cs)
                p.wait("vector", "v", cvg)
                cvn = p.op("vector", lambda e: e.tensor_mul(
                    rtmp[:], rtmp[:], gtile[:]), inc="v")
                p.wait("vector", "v", cvn)
                cvn = p.op("vector", lambda e: e.tensor_tensor(
                    xn[:], rtmp[:], btile[:], ALU.add), inc="v")
                ev_f = cvn
                cvx = tr.run(lambda c: xn[:, c * 128:(c + 1) * 128], 8,
                             fc.attnT[:], extra_waits=[("v", cvn)])
                p.wait("gpsimd", "v", cvx)
                p.wait("gpsimd", "dsync", xdma)
                xdma = p.dma("gpsimd", xnT_scr[:, :, t * 128:(t + 1) * 128],
                             fc.attnT[:].rearrange("p (c n) -> p c n", n=128))
            p.wait("gpsimd", "t", p.counts["t"])
            p.wait("gpsimd", "v", p.counts["v"])
            p.wait("sync", "t", p.counts["t"])
            p.wait("sync", "v", p.counts["v"])
            p.wait("sync", "dgp", xdma)
            p.wait("vector", "dgp", xdma)
            p.wait("vector", "t", p.counts["t"])
            p.wait("tensor", "s", p.counts["s"])
            cza = None
            for t in range(NT):
                cza = p.op("vector", lambda e, t=t: e.memset(accv(t), 0.0),
                           inc="v" if t == NT - 1 else None)
            gelu_s = None
            gelu_par = [None, None]
            cag_par = [None, None]
            acc_v = cza
            wld = None
            for ci, (j0, jw) in enumerate(HCH):
                jreal = min(jw, 2730 - j0)
                nblk = jw // 128
                pw = ci % 2
                p.wait("gpsimd", "t", wld)
                cwa = p.dma("gpsimd", w1a[pw][:, :, 0:jreal],
                            W1[li].rearrange("(c p) m -> p c m", p=128)
                            [:, :, j0:j0 + jreal])
                cwg = p.dma("gpsimd", w1g[pw][:, :, 0:jreal],
                            W1[li].rearrange("(c p) m -> p c m", p=128)
                            [:, :, 2730 + j0:2730 + j0 + jreal])
                if j0 + jw <= 2688:
                    cw2c = p.dma("gpsimd", w2c[pw][:, 0:nblk, :],
                                 W2[li][j0:j0 + jw].rearrange(
                                     "(c p) m -> p c m", p=128))
                else:
                    czz = p.op("gpsimd", lambda e, pw=pw: e.memset(
                        w2c[pw].rearrange("p c m -> p (c m)")[:, 0:2048], 0.0),
                        inc="g")
                    cw2c = p.dma("gpsimd", w2c[pw][:, 0:1, :],
                                 W2[li][2560:2688].rearrange(
                                     "(c p) m -> p c m", p=128))
                    cw2c = p.dma("gpsimd", w2c[pw][0:42, 1, :],
                                 W2[li][2688:2730])
                    p.wait("tensor", "g", czz)
                p.wait("tensor", "dgp", cw2c)
                xb2 = None
                for t in range(NT):
                    bb = t % 2
                    pa = [ps_a, fc.ps_s[0]][bb]
                    pg = [ps_g, fc.ps_s[1]][bb]
                    p.wait("sync", "t", xb2)
                    cxt = p.dma("sync", xnt_in[t % 2],
                                xnT_scr[:, :, t * 128:(t + 1) * 128])
                    p.wait("tensor", "dsync", cxt)
                    p.wait("tensor", "s", gelu_par[bb])
                    p.wait("tensor", "v", cag_par[bb])
                    cma = None
                    for c in range(8):
                        cma = p.op("tensor", lambda e, c=c, t=t, jreal=jreal,
                                   pw=pw, pa=pa: e.matmul(
                                       pa[:, 0:jreal],
                                       xnt_in[t % 2][:, c, :],
                                       w1a[pw][:, c, 0:jreal],
                                       start=(c == 0), stop=False), inc=None)
                    cma = p.op("tensor", lambda e, j0=j0, jreal=jreal, pa=pa:
                               e.matmul(
                                   pa[:, 0:jreal], onesr[0:1, 0:128],
                                   b1r[0:1, j0:j0 + jreal],
                                   start=False, stop=True), inc="t")
                    cmg = None
                    for c in range(8):
                        cmg = p.op("tensor", lambda e, c=c, t=t, jreal=jreal,
                                   pw=pw, pg=pg: e.matmul(
                                       pg[:, 0:jreal],
                                       xnt_in[t % 2][:, c, :],
                                       w1g[pw][:, c, 0:jreal],
                                       start=(c == 0), stop=False), inc=None)
                    cmg = p.op("tensor", lambda e, j0=j0, jreal=jreal, pg=pg:
                               e.matmul(
                                   pg[:, 0:jreal], onesr[0:1, 0:128],
                                   b1r[0:1, 2730 + j0:2730 + j0 + jreal],
                                   start=False, stop=True), inc="t")
                    xb2 = cmg
                    p.wait("scalar", "t", cmg)
                    gelu_s = p.op("scalar", lambda e, jreal=jreal, pg=pg:
                                  e.activation(
                                      rtmp[:, 0:jreal], pg[:, 0:jreal],
                                      AF.Gelu), inc="s")
                    gelu_par[bb] = gelu_s
                    p.wait("vector", "s", gelu_s)
                    p.wait("vector", "t", cma)
                    cag = p.op("vector", lambda e, jreal=jreal, pa=pa:
                               e.tensor_tensor(
                                   agb[:, 0:jreal], pa[:, 0:jreal],
                                   rtmp[:, 0:jreal], ALU.mult), inc="v")
                    cag_par[bb] = cag
                    if jreal < jw:
                        p.wait("vector", "v", cag)
                        cag = p.op("vector", lambda e, jreal=jreal, jw=jw:
                                   e.memset(agb[:, jreal:jw], 0.0), inc="v")
                    cvt = tr.run(lambda c: agb[:, c * 128:(c + 1) * 128], nblk,
                                 hT, extra_waits=[("v", cag)])
                    p.wait("tensor", "v", cvt)
                    p.wait("tensor", "v", acc_v)
                    cm2 = None
                    for fb in range(2):
                        for c in range(nblk):
                            cm2 = p.op("tensor", lambda e, c=c, fb=fb, pw=pw,
                                       nblk=nblk: e.matmul(
                                           ps_y[fb][:],
                                           hT[:, c * 128:(c + 1) * 128],
                                           w2c[pw][:, c, fb * 512:
                                                   (fb + 1) * 512],
                                           start=(c == 0),
                                           stop=(c == nblk - 1)),
                                       inc="t" if c == nblk - 1 else None)
                    p.wait("vector", "t", cm2)
                    for fb in range(2):
                        acc_v = p.op("vector", lambda e, t=t, fb=fb: e.tensor_tensor(
                            accv(t)[:, fb * 512:(fb + 1) * 512], ps_y[fb][:],
                            accv(t)[:, fb * 512:(fb + 1) * 512], ALU.add),
                            inc="v" if fb == 1 else None)
                wld = cm2
            b2t_v = None
            p.wait("tensor", "v", acc_v)
            cmb1 = p.op("tensor", lambda e: e.matmul(
                ps_y[0][:], onesr[0:1, 0:128], b2r[0:1, 0:512],
                start=True, stop=True), inc="t")
            cmb2 = p.op("tensor", lambda e: e.matmul(
                ps_y[1][:], onesr[0:1, 0:128], b2r[0:1, 512:1024],
                start=True, stop=True), inc="t")
            p.wait("vector", "t", cmb2)
            p.op("vector", lambda e: e.tensor_copy(gtile[:, 0:512], ps_y[0][:]))
            cbt = p.op("vector", lambda e: e.tensor_copy(
                gtile[:, 512:1024], ps_y[1][:]), inc="v")
            p.wait("vector", "v", cbt)
            zdma = None
            p.wait("scalar", "dsync", p.counts["dsync"])
            for t in range(NT):
                b = t % 2
                p.wait("scalar", "v", ev_f)
                p.wait("scalar", "dsync", zdma)
                cz = p.dma("scalar", yt[b][:], zin[t * 128:(t + 1) * 128, :])
                p.wait("vector", "dscalar", cz)
                cvr = p.op("vector", lambda e, b=b, t=t: e.tensor_add(
                    rtmp[:], accv(t), gtile[:]), inc="v")
                p.wait("vector", "v", cvr)
                ev_f = p.op("vector", lambda e, b=b: e.tensor_add(
                    yt[b][:], yt[b][:], rtmp[:]), inc="v")
                p.wait("sync", "v", ev_f)
                zdma = p.dma("sync", zout[t * 128:(t + 1) * 128, :], yt[b][:])
            zin, zout = zout, zin

        p.wait("sync", "v", p.counts["v"])
        p.wait("scalar", "dsync", p.counts["dsync"])
        out_dma = [None] * NT
        for t in range(NT):
            b = t % 2
            if t >= 2:
                p.wait("sync", "v", ev_f)
                p.wait("sync", "dsync", out_dma[t - 2])
                p.wait("scalar", "dsync", out_dma[t - 2])
            cx = p.dma("sync", xt[b][:], x_in[t * 128:(t + 1) * 128, :])
            cz = p.dma("scalar", yt[b][:], zin[t * 128:(t + 1) * 128, :])
            p.wait("vector", "dsync", cx)
            p.wait("vector", "dscalar", cz)
            ev_f = p.op("vector", lambda e, b=b: e.scalar_tensor_tensor(
                yt[b][:], yt[b][:], 0.5, xt[b][:], ALU.mult, ALU.add), inc="v")
            p.wait("sync", "v", ev_f)
            out_dma[t] = p.dma("sync", x_out[t * 128:(t + 1) * 128, :],
                               yt[b][:])
        p.wait("sync", "dsync", p.counts["dsync"])
        p.wait("sync", "dscalar", p.counts["dscalar"])
        if fused is None:
            p.emit()
    return decls




def build_fused(nc):
    x0 = nc.dram_tensor("x0", [T_SEQ, D], F32, kind="ExternalInput").ap()
    y = nc.dram_tensor("y", [T_SEQ, D], F32, kind="ExternalOutput").ap()
    xA = nc.dram_tensor("xA", [T_SEQ, D], F32, kind="Internal").ap()
    xB = nc.dram_tensor("xB", [T_SEQ, D], F32, kind="Internal").ap()
    w1p = nc.dram_tensor("w1p_i", [D, 4096], F32, kind="Internal").ap()
    b1p = nc.dram_tensor("b1p_i", [1, 4096], F32, kind="Internal").ap()
    w2p = nc.dram_tensor("w2p_i", [4096, D], F32, kind="Internal").ap()
    b2p = nc.dram_tensor("b2p_i", [1, D], F32, kind="Internal").ap()

    stack = contextlib.ExitStack()
    p = Prog(nc, stack)
    build_l0fold(nc, fused=p, pfx="f_",
                 io={"w1p": w1p, "b1p": b1p, "w2p": w2p, "b2p": b2p})
    mio = {"w1": w1p, "b1": b1p.squeeze(0), "w2": w2p, "b2": b2p.squeeze(0)}
    build_mlp_ffn(nc, fused=p, pfx="s01_",
                  io=dict(mio, x_in=x0, x_out=xA))
    build_l0attn(nc, fused=p, pfx="s02_", io={"x_in": xA, "x_out": xB})
    build_mlp_ffn(nc, fused=p, pfx="s03_",
                  io=dict(mio, x_in=xB, x_out=xA))
    d4 = build_l1moe(nc, fused=p, pfx="s04_", io={"x_in": xA, "x_out": xB})
    build_l1attn(nc, fused=p, pfx="s05_", io={"x_in": xB, "x_out": xA})
    w4 = {k: d4[k] for k in ("rW", "rb", "W1", "b1", "W2", "b2")}
    build_l1moe(nc, fused=p, pfx="s06_", io=dict(w4, x_in=xA, x_out=xB))
    d7 = build_mlp_ffn(nc, fused=p, pfx="s07_", io={"x_in": xB, "x_out": xA})
    build_l2attn(nc, fused=p, pfx="s08_", io={"x_in": xA, "x_out": xB})
    w7 = {k: d7[k] for k in ("w1", "b1", "w2", "b2")}
    build_mlp_ffn(nc, fused=p, pfx="s09_", io=dict(w7, x_in=xB, x_out=xA))
    d10 = build_l3ffn(nc, fused=p, pfx="s10_", io={"x_in": xA, "x_out": xB})
    build_l3attn(nc, fused=p, pfx="s11_", io={"x_in": xB, "x_out": xA})
    w10 = {k: d10[k] for k in ("Wqkv", "q_scale", "k_scale", "Wo", "ln_g",
                               "ln_b", "W1", "b1", "W2", "b2")}
    build_l3ffn(nc, fused=p, pfx="s12_", io=dict(w10, x_in=xA, x_out=y))
    p.barrier()
    p.emit()
    stack.close()
    return nc


def kernel(**d):
    x = np.ascontiguousarray(d["L0_attn_x"], dtype=np.float32)
    nc = _NC_CACHE.get("fused")
    if nc is None:
        nc = bass.Bass("TRN2", target_bir_lowering=False)
        build_fused(nc)
        _NC_CACHE["fused"] = nc
    w = {
        "f_W_exp": d["L0_ffn_W_exp"], "f_b_exp": d["L0_ffn_b_exp"],
        "f_upw": np.ascontiguousarray(d["L0_ffn_up_weights"]).reshape(12, 8192),
        "f_downw": np.ascontiguousarray(d["L0_ffn_down_weights"]).reshape(12, 8192),
        "f_W_con": d["L0_ffn_W_con"], "f_up_bias": d["L0_ffn_up_bias"],
        "f_b_con": d["L0_ffn_b_con"], "f_down_bias": d["L0_ffn_down_bias"],
        "s02_Wq": d["L0_attn_Wq"], "s02_Wk": d["L0_attn_Wk"],
        "s02_Wv": d["L0_attn_Wv"], "s02_Wo": d["L0_attn_Wo"],
        "s04_rW": d["L1_ffn_router_W"], "s04_rb": d["L1_ffn_router_b"],
        "s04_W1": d["L1_ffn_W1"], "s04_b1": d["L1_ffn_b1"],
        "s04_W2": d["L1_ffn_W2"], "s04_b2": d["L1_ffn_b2"],
        "s05_Wqkv": d["L1_attn_Wqkv"], "s05_bqkv": d["L1_attn_bqkv"],
        "s05_Wout": d["L1_attn_Wout"], "s05_bout": d["L1_attn_bout"],
        "s07_w1": d["L2_ffn_w1"], "s07_b1": d["L2_ffn_b1"],
        "s07_w2": d["L2_ffn_w2"], "s07_b2": d["L2_ffn_b2"],
        "s08_Wq": d["L2_attn_Wq"], "s08_Wv": d["L2_attn_Wv"],
        "s08_Wc": d["L2_attn_Wc"], "s08_Wp1": d["L2_attn_Wp1"],
        "s08_Wp2": d["L2_attn_Wp2"], "s08_idv": d["L2_attn_identity"],
        "s10_Wqkv": d["L3_ffn_Wqkv"], "s10_q_scale": d["L3_ffn_q_scale"],
        "s10_k_scale": d["L3_ffn_k_scale"], "s10_Wo": d["L3_ffn_Wo"],
        "s10_ln_g": d["L3_ffn_ln_g"], "s10_ln_b": d["L3_ffn_ln_b"],
        "s10_W1": d["L3_ffn_W1"], "s10_b1": d["L3_ffn_b1"],
        "s10_W2": d["L3_ffn_W2"], "s10_b2": d["L3_ffn_b2"],
        "s11_Wq": d["L3_attn_Wq"], "s11_bq": d["L3_attn_bq"],
        "s11_Wk": d["L3_attn_Wk"], "s11_bk": d["L3_attn_bk"],
        "s11_Wv": d["L3_attn_Wv"], "s11_bv": d["L3_attn_bv"],
        "s11_Wo": d["L3_attn_Wo"], "s11_bo": d["L3_attn_bo"],
    }
    maps = [dict(w, x0=x[b]) for b in range(8)]
    res = run_bass_kernel_spmd(nc, maps, core_ids=list(range(8)))
    return np.stack([np.asarray(res.results[b]["y"]) for b in range(8)], axis=0)
